# version 13
# baseline (speedup 1.0000x reference)
import numpy as np
import ml_dtypes
from contextlib import ExitStack

import concourse.bass as bass
import concourse.mybir as mybir
from concourse.bass_utils import run_bass_kernel_spmd

F32 = mybir.dt.float32
BF16 = mybir.dt.bfloat16
AF = mybir.ActivationFunctionType
ALU = mybir.AluOpType
AX = mybir.AxisListType
BF = ml_dtypes.bfloat16

NCORES = 8
T = 1024
DM = 2048
NCH = DM // 128
H, D = 32, 64
E = 16
I_E = 1408
I_S = 2816
NKE = I_E // 128
NKS = I_S // 128
EPS = 1e-5
SM_SCALE = 0.125

TS = 128
NSLAB = T // TS
GTOK = 16
NGRP = TS // GTOK


class SemC:
    __slots__ = ("h", "n")

    def __init__(self, h):
        self.h = h
        self.n = 0


def build_program(taps=(), stop=None, p3_slabs=NSLAB, p3_mode='full'):
    nc = bass.Bass("TRN2", target_bir_lowering=False)
    taps = set(taps)

    def din(name, shape, dt=F32):
        return nc.dram_tensor(name, list(shape), dt, kind="ExternalInput").ap()

    x_in = din("x_in", (T, DM))
    wq_d = din("wq_b", (NCH, 128, DM), BF16)
    wk_d = din("wk_b", (NCH, 128, DM), BF16)
    wv_d = din("wv_b", (NCH, 128, DM), BF16)
    wo_d = din("wo_b", (NCH, 128, DM), BF16)
    gam_d = din("gamma_pp", (128, NCH))
    bet_d = din("beta_pp", (128, NCH))
    gate_d = din("gate_wT", (DM, E), BF16)
    sw1_d = din("sw1_b", (2 * NKS, 128, DM), BF16)
    sw2_d = din("sw2_b", (NCH, 128, I_S), BF16)
    w1_d = din("w1_b", (E, 2 * NKE, 128, DM), BF16)
    w2_d = din("w2_b", (E, NCH, 128, I_E), BF16)
    id_d = din("ident", (128, 128))
    ones_d = din("ones_in", (128, 128))
    sel_d = din("sel", (E, E * 128), BF16)
    cosf_d = din("cosf", (128, H * T // 2), BF16)
    sinf_d = din("sinf", (128, H * T // 2), BF16)

    out_d = nc.dram_tensor("out", [T, DM], F32, kind="ExternalOutput").ap()

    q_scr = nc.dram_tensor("q_scr", [DM, T], BF16, kind="Internal").ap()
    k_scr = nc.dram_tensor("k_scr", [DM, T], BF16, kind="Internal").ap()
    v_scr = nc.dram_tensor("v_scr", [DM, T], BF16, kind="Internal").ap()
    o_scr = nc.dram_tensor("o_scr", [DM, T], BF16, kind="Internal").ap()
    x_spill = nc.dram_tensor("x_spill", [128, NCH * T], F32, kind="Internal").ap()

    tap_t = {}
    tap_shapes = {
        "a": ((128, NCH * T), BF16), "f": ((128, NCH * T), BF16),
        "ao": ((128, NCH * T), BF16), "x1": ((128, NCH * T), F32),
        "mid": ((128, NKS * T), BF16), "moe": ((128, NCH * T), F32),
        "cmb": ((E, T), BF16), "qs": ((DM, T), BF16), "ks": ((DM, T), BF16),
        "vs": ((DM, T), BF16), "os": ((DM, T), BF16), "logits": ((128, 128), F32),
        "rot": ((128, 4096), BF16), "osl": ((64, H * TS), BF16),
        "eb": ((64, 512), BF16), "ds": ((64, 512), F32),
    }
    for name in taps:
        shp, dt = tap_shapes[name]
        tap_t[name] = nc.dram_tensor("tap_" + name, list(shp), dt,
                                     kind="ExternalOutput").ap()

    st = ExitStack()
    all_stacks = []

    def mkstack():
        s_ = ExitStack()
        all_stacks.append(s_)
        return s_

    def sbuf(name, shape, dt, stack=None):
        return (stack or st).enter_context(nc.sbuf_tensor(name, shape, dt)).ap()

    PS = [st.enter_context(nc.psum_tensor(f"psb{i}", [128, 512], F32)).ap()
          for i in range(8)]

    sems = {}
    for s in ("sW", "sPE", "sEV", "sEV2", "sSQ", "sST", "sFLD", "sCAST",
              "sSL", "sSLg", "sROPE", "sGR", "sSUB", "sEXP", "sSM", "sPV",
              "sOEV", "sODMA", "sQD", "sXSP", "sOUT", "sAUX", "sCMB",
              "sTAP", "sSG", "sDN", "sWD", "sVC", "sW2", "sQD2", "sWD2",
              "sOUT2", "sCV", "sXL", "sXL2", "sWD3", "sSL2"):
        sems[s] = SemC(st.enter_context(nc.semaphore(s)))

    te, ve, sc, gp, sy = nc.tensor, nc.vector, nc.scalar, nc.gpsimd, nc.sync
    ALL_ENG = (te, ve, sc, gp, sy)

    def S(n):
        return sems[n]

    def inc(sm, inst, k=1):
        inst.then_inc(sm.h, k)
        sm.n += k
        return inst

    def w(eng, sm, tgt=None):
        eng.wait_ge(sm.h, sm.n if tgt is None else tgt)

    def join(engines, sem_names):
        for e_ in engines:
            for sn in sem_names:
                w(e_, S(sn))

    def esync(eng, inst):
        inc(S("sVC"), inst)
        w(eng, S("sVC"))
        return inst

    def vsync(inst):
        return esync(ve, inst)

    _scope = [None]

    def ph(name):
        if _scope[0] is not None:
            nc.leave_named_scope(_scope[0][0], _scope[0][1], False)
        _scope[0] = (name, nc.enter_named_scope(name, False)[0])

    ident = sbuf("identsb", [128, 128], F32)
    identb = sbuf("identb", [128, 128], BF16)
    ones_bf = sbuf("ones_bf", [128, 128], BF16)
    gam = sbuf("gam", [128, NCH], F32)
    bet = sbuf("bet", [128, NCH], F32)
    gate_sb = sbuf("gate_sb", [128, E * NCH], BF16)
    TNa = sbuf("TNa", [65, 1024], F32)
    TNb = sbuf("TNb", [65, 1024], F32)
    TNc = sbuf("TNc", [65, 1024], F32)
    TNbf = sbuf("TNbf", [1, 2048], BF16)
    FB0 = sbuf("FB0", [128, 1024], BF16)
    FB1 = sbuf("FB1", [128, 1024], BF16)
    CM0 = sbuf("CM0", [128, 1024], BF16)
    CM1 = sbuf("CM1", [128, 1024], BF16)
    SQ = [sbuf(f"SQst{i}", [128, 1024], BF16) for i in range(2)]
    ACT = sbuf("ACT", [128, NCH * T], BF16)
    WB = [sbuf(f"WB{i}", [128, NKS * 128], BF16) for i in range(2)]
    WD = [sbuf(f"WD{i}", [128, NKE * 128], BF16) for i in range(3)]

    def afull(c):
        return ACT[:, c * T:(c + 1) * T]

    def achunk(c, nh):
        return ACT[:, c * T + nh * 512: c * T + (nh + 1) * 512]

    def emit_ln_fields(chunk_full, chunk_half, ready, ready_base, double,
                       f32_fields=None, ready_step=1):
        bsq, bst = S("sSQ").n, S("sST").n
        for c in range(NCH):
            w(sc, S(ready), ready_base + (c + 1) * ready_step)
            if c >= 2:
                w(sc, S("sST"), bst + (c - 1))
            inc(S("sSQ"), sc.activation(SQ[c % 2][:, :], chunk_full(c), AF.Square))
        for c in range(NCH):
            w(te, S(ready), ready_base + (c + 1) * ready_step)
            w(te, S("sSQ"), bsq + c + 1)
            for nh in range(2):
                te.matmul(PS[4 + nh][0:1, :], ones_bf[:, 0:1], chunk_half(c, nh),
                          start=(c == 0), stop=(c == NCH - 1))
            for nh in range(2):
                mm = te.matmul(PS[4 + nh][32:33, :], ones_bf[:, 0:1],
                               SQ[c % 2][:, nh * 512:(nh + 1) * 512],
                               start=(c == 0), stop=(c == NCH - 1))
            inc(S("sST"), mm)
        w(ve, S("sST"))
        a0 = TNa[0:1, :]
        a32 = TNa[32:33, :]
        b0 = TNb[0:1, :]
        b32 = TNb[32:33, :]
        c0 = TNc[0:1, :]
        c32 = TNc[32:33, :]
        ve.tensor_copy(TNa[0:1, 0:512], PS[4][0:1, :])
        ve.tensor_copy(TNa[0:1, 512:1024], PS[5][0:1, :])
        ve.tensor_copy(TNa[32:33, 0:512], PS[4][32:33, :])
        ve.tensor_copy(TNa[32:33, 512:1024], PS[5][32:33, :])
        mu, cc = b0, c0
        ve.tensor_scalar(mu, a0, 1.0 / DM, None, ALU.mult)
        ve.tensor_scalar(c0, a32, 1.0 / DM, None, ALU.mult)
        ve.tensor_mul(a0, mu, mu)
        ve.tensor_sub(c0, c0, a0)
        i1 = ve.tensor_scalar(a0, c0, EPS, None, ALU.add)
        inc(S("sEV"), i1)
        w(sc, S("sEV"))
        if double:
            esync(sc, sc.activation(b32, a0, AF.Ln))
            sc.activation(a32, b32, AF.Exp, scale=-0.5)
            i2 = sc.activation(a0, b32, AF.Exp, scale=-1.0)
            inc(S("sEXP"), i2)
            w(ve, S("sEXP"))
            ve.tensor_mul(a0, c0, a0)
            i3 = ve.tensor_scalar(a0, a0, EPS, None, ALU.add)
            inc(S("sEV"), i3)
            w(sc, S("sEV"))
            esync(sc, sc.activation(b32, a0, AF.Ln))
            i4 = sc.activation(c32, b32, AF.Exp, scale=-0.5)
            inc(S("sEXP"), i4)
            w(ve, S("sEXP"))
            ve.tensor_mul(cc, a32, c32)
        else:
            esync(sc, sc.activation(b32, a0, AF.Ln))
            i2 = sc.activation(cc, b32, AF.Exp, scale=-0.5)
            inc(S("sEXP"), i2)
            w(ve, S("sEXP"))
        ve.tensor_mul(a0, mu, cc)
        ve.tensor_copy(TNbf[0:1, 0:1024], cc)
        i5 = ve.tensor_copy(TNbf[0:1, 1024:2048], a0)
        inc(S("sEV"), i5)
        w(te, S("sEV"))
        te.matmul(PS[6][:, :], ones_bf[0:1, :], TNbf[0:1, 0:512],
                  start=True, stop=True)
        te.matmul(PS[7][:, :], ones_bf[0:1, :], TNbf[0:1, 512:1024],
                  start=True, stop=True)
        te.matmul(PS[4][:, :], ones_bf[0:1, :], TNbf[0:1, 1024:1536],
                  start=True, stop=True)
        i6 = te.matmul(PS[5][:, :], ones_bf[0:1, :], TNbf[0:1, 1536:2048],
                       start=True, stop=True)
        inc(S("sST"), i6)
        w(ve, S("sST"))
        ve.tensor_copy(FB0[:, 0:512], PS[6][:, :])
        ve.tensor_copy(FB0[:, 512:1024], PS[7][:, :])
        ve.tensor_copy(FB1[:, 0:512], PS[4][:, :])
        i7 = ve.tensor_copy(FB1[:, 512:1024], PS[5][:, :])
        if f32_fields is not None:
            FF0, FF1 = f32_fields
            ve.tensor_copy(FF0[:, 0:512], PS[6][:, :])
            ve.tensor_copy(FF0[:, 512:1024], PS[7][:, :])
            ve.tensor_copy(FF1[:, 0:512], PS[4][:, :])
            i7 = ve.tensor_copy(FF1[:, 512:1024], PS[5][:, :])
        inc(S("sFLD"), i7)

    def run_proj(blocks_dram, nk, rhs_fn, consume, wbufs=None, dma_eng=None):
        wbufs = wbufs or WB
        dma_eng = dma_eng or gp
        nblk = len(blocks_dram)
        w(te, S("sEV"))
        w(dma_eng, S("sPE"))
        wsem = (S("sW"), S("sW2"))
        bwp = (wsem[0].n, wsem[1].n)
        bp, be = S("sPE").n, S("sEV").n
        for m in range(nblk):
            if m >= 2:
                w(dma_eng, S("sPE"), bp + m - 1)
            inc(wsem[m % 2], dma_eng.dma_start(
                wbufs[m % 2][:, 0:nk * 128], blocks_dram[m],
                max_dma_last_dim=8192), 16)
        for m in range(nblk):
            w(te, wsem[m % 2], bwp[m % 2] + 16 * (m // 2 + 1))
            if m >= 2:
                w(te, S("sEV"), be + m - 1)
            pA, pB = PS[2 * (m % 2)], PS[2 * (m % 2) + 1]
            for k in range(nk):
                lhs = wbufs[m % 2][:, k * 128:(k + 1) * 128]
                te.matmul(pA, lhs, rhs_fn(k, 0), start=(k == 0), stop=(k == nk - 1))
                mm = te.matmul(pB, lhs, rhs_fn(k, 1), start=(k == 0),
                               stop=(k == nk - 1))
            inc(S("sPE"), mm)
        for m in range(nblk):
            consume(m, bp, be)

    def emit_tap(name, src_ap, wait_sems):
        if name not in tap_t:
            return
        for sn in wait_sems:
            w(sy, S(sn))
        inc(S("sTAP"), sy.dma_start(tap_t[name][:, :], src_ap), 16)

    def finish():
        if _scope[0] is not None:
            nc.leave_named_scope(_scope[0][0], _scope[0][1], False)
            _scope[0] = None
        join((sy,), ["sOUT", "sOUT2", "sTAP"])
        join((te, ve, sc, gp), ["sOUT", "sOUT2"])
        for stk in reversed(all_stacks):
            stk.close()
        st.close()
        return nc

    BIGA = sbuf("BIGA", [128, NCH * T], F32)
    X = BIGA

    def xchunk(c):
        return X[:, c * T:(c + 1) * T]

    ph("P0_in")
    stP0 = mkstack()
    TM = [sbuf(f"TM{i}", [128, DM], F32, stP0) for i in range(2)]

    inc(S("sAUX"), sy.dma_start(ident, id_d), 16)
    inc(S("sAUX"), sy.dma_start(gam, gam_d), 16)
    inc(S("sAUX"), sy.dma_start(bet, bet_d), 16)
    inc(S("sCAST"), gp.dma_start(identb, id_d), 16)
    inc(S("sCAST"), gp.dma_start(ones_bf, ones_d), 16)
    inc(S("sCAST"), gp.dma_start(
        gate_sb.rearrange("p (k e) -> p k e", k=NCH),
        gate_d.rearrange("(k p) e -> p k e", p=128)), 16)

    qsem = (S("sQD"), S("sQD2"))
    bq0p = (qsem[0].n, qsem[1].n)
    bp0, be0 = S("sPE").n, S("sEV").n
    for j in range(NSLAB):
        if j >= 2:
            w(sy, S("sPE"), bp0 + j - 1)
        inc(qsem[j % 2], sy.dma_start(TM[j % 2], x_in[j * 128:(j + 1) * 128, :]), 16)
    for j in range(NSLAB):
        w(te, qsem[j % 2], bq0p[j % 2] + 16 * (j // 2 + 1))
        if j == 0:
            w(te, S("sAUX"))
        if j >= 1:
            w(te, S("sEV"), be0 + j)
        for c in range(NCH):
            mm = te.transpose(PS[4 + c % 4][:, (c // 4) * 128:(c // 4) * 128 + 128],
                              TM[j % 2][:, c * 128:(c + 1) * 128], ident)
        inc(S("sPE"), mm)
        w(ve, S("sPE"), bp0 + j + 1)
        for c in range(NCH):
            i0 = ve.tensor_copy(xchunk(c)[:, j * 128:(j + 1) * 128],
                                PS[4 + c % 4][:, (c // 4) * 128:(c // 4) * 128 + 128])
        inc(S("sEV"), i0)
    join(ALL_ENG, ["sEV", "sQD", "sQD2", "sAUX", "sCAST"])
    stP0.close()
    if stop == "P0":
        return finish()

    ph("P1_preLN")
    bc1 = S("sCV").n
    for c in range(NCH):
        inc(S("sCV"), ve.tensor_copy(afull(c), xchunk(c)))
    emit_ln_fields(afull, achunk, "sCV", bc1, double=True)
    join((ve,), ["sFLD"])
    for c in range(NCH):
        ve.tensor_mul(CM0[:, :], afull(c), FB0)
        ve.tensor_sub(CM0[:, :], CM0[:, :], FB1)
        i0 = ve.tensor_scalar(afull(c), CM0[:, :], gam[:, c:c + 1],
                              bet[:, c:c + 1], ALU.mult, ALU.add)
    inc(S("sEV"), i0)
    join(ALL_ENG, ["sEV"])
    emit_tap("a", ACT[:, :], ["sEV"])
    if stop == "P1":
        return finish()

    ph("P2_qkv")
    stP2 = mkstack()
    QS = [sbuf(f"QSst{i}", [128, 1024], BF16, stP2) for i in range(2)]

    for wmat, scr in ((wq_d, q_scr), (wk_d, k_scr), (wv_d, v_scr)):
        qsem2 = (S("sQD"), S("sQD2"))
        bqdp = (qsem2[0].n, qsem2[1].n)

        def consume(m, bp, be, scr=scr, qsem2=qsem2, bqdp=bqdp):
            w(ve, S("sPE"), bp + m + 1)
            w(ve, qsem2[m % 2], bqdp[m % 2] + 16 * (m // 2))
            ve.tensor_copy(QS[m % 2][:, 0:512], PS[2 * (m % 2)])
            i = ve.tensor_copy(QS[m % 2][:, 512:1024], PS[2 * (m % 2) + 1])
            inc(S("sEV"), i)
            w(sy, S("sEV"))
            inc(qsem2[m % 2],
                sy.dma_start(scr[m * 128:(m + 1) * 128, :], QS[m % 2]), 16)

        run_proj([wmat[m] for m in range(NCH)], NCH, achunk, consume)
    join(ALL_ENG, ["sQD", "sQD2", "sEV"])
    stP2.close()
    emit_tap("qs", q_scr[:, :], ["sQD", "sQD2"])
    emit_tap("ks", k_scr[:, :], ["sQD", "sQD2"])
    emit_tap("vs", v_scr[:, :], ["sQD", "sQD2"])
    if stop == "P2":
        return finish()

    ph("P3_attn")
    stP3 = mkstack()
    NGR3 = 4
    QKR = ACT[:, 0:2048]
    QKP = ACT[:, 2048:4096]
    KKP = ACT[:, 4096:6144]
    CSF = ACT[:, 6144:8192]
    SNF = ACT[:, 8192:10240]
    KKR = [ACT[:, 10240 + 2048 * i: 12288 + 2048 * i] for i in range(2)]
    QPK = [sbuf(f"QPKs{i}", [128, 4096], BF16, stP3) for i in range(2)]
    VS2 = [sbuf(f"VS2s{i}", [64, 4096], BF16, stP3) for i in range(2)]
    OSL = sbuf("OSL", [64, H * TS], BF16, stP3)
    ETt = [sbuf(f"ETts{i}", [64, 512], BF16, stP3) for i in range(2)]
    ET2 = [sbuf(f"ET2s{i}", [64, 1024], BF16, stP3) for i in range(2)]
    DS = [sbuf(f"DSs{i}", [64, 512], F32, stP3) for i in range(2)]
    EB = [sbuf(f"EBs{i}", [64, 512], BF16, stP3) for i in range(2)]
    MXZ = sbuf("MXZ", [64, 64], F32, stP3)

    q3 = q_scr.rearrange("(h d) t -> d h t", d=D)
    k3 = k_scr.rearrange("(h d) t -> d h t", d=D)
    qp4 = q_scr.rearrange("(h d2 p2) t -> p2 d2 h t", d2=32, p2=2)
    kp4 = k_scr.rearrange("(h d2 p2) t -> p2 d2 h t", d2=32, p2=2)
    v3 = v_scr.rearrange("(h d) t -> h d t", d=D)
    o3 = o_scr.rearrange("(h d) t -> d h t", d=D)
    cs4 = cosf_d.rearrange("p (h s u) -> p h s u", h=H, s=NSLAB)
    sn4 = sinf_d.rearrange("p (h s u) -> p h s u", h=H, s=NSLAB)

    QKRv = QKR.rearrange("p (h u) -> p u h", h=H)
    QKPv = QKP.rearrange("p (h u) -> p u h", h=H)
    QPK4 = [QPK[i].rearrange("p (u two h) -> p u two h", two=2, h=H)
            for i in range(2)]
    KKRv = [KKR[i].rearrange("p (h u) -> p u h", h=H) for i in range(2)]
    VS2v = [VS2[i].rearrange("p (d u) -> p d u", d=D) for i in range(2)]
    ET2v = [ET2[i].rearrange("p (u two h) -> p u two h", two=2, h=H)
            for i in range(2)]
    ETtv = [ETt[i].rearrange("p (u h) -> p u h", h=H) for i in range(2)]
    OSL3 = OSL.rearrange("p (h t) -> p h t", h=H)
    OSL4 = OSL.rearrange("p (h two u) -> p h two u", h=H, two=2)

    bsl = S("sSL").n
    bsl2 = S("sSL2").n
    bslg = S("sSLg").n
    brope = S("sROPE").n
    bgr = S("sGR").n
    bsub = S("sSUB").n
    bexp = S("sEXP").n
    bsm = S("sSM").n
    bpv = S("sPV").n
    boev = S("sOEV").n
    bodma = S("sODMA").n

    for i in range(2):
        ve.memzero(QPK4[i][64:128, :, 0, :])
        ve.memzero(QPK4[i][0:64, :, 1, :])
        ve.memzero(ET2v[i][32:64, :, 0, :])
        i0 = ve.memzero(ET2v[i][0:32, :, 1, :])
    vsync(i0)

    def hv(ap):
        return ap.rearrange("p (h u) -> p h u", h=H)

    def emit_loads(s):
        tsl0 = slice(s * TS, s * TS + 64)
        tsl1 = slice(s * TS + 64, (s + 1) * TS)
        if s >= 1:
            w(sy, S("sROPE"), brope + s)
            w(sc, S("sROPE"), brope + s)
            w(gp, S("sROPE"), brope + s)
        if s >= 2:
            w(sc, S("sGR"), bgr + NGR3 * (s - 1))
            w(gp, S("sPV"), bpv + NGR3 * (s - 1))
        inc(S("sSL"), sy.dma_start(hv(QKR[0:64, :]), q3[:, :, tsl0]), 16)
        inc(S("sSL"), sy.dma_start(hv(QKR[64:128, :]), q3[:, :, tsl1]), 16)
        inc(S("sSL"), sy.dma_start(hv(QKP[0:32, :]), qp4[1][:, :, tsl0]), 16)
        inc(S("sSL"), sy.dma_start(hv(QKP[32:64, :]), qp4[0][:, :, tsl0]), 16)
        inc(S("sSL"), sy.dma_start(hv(QKP[64:96, :]), qp4[1][:, :, tsl1]), 16)
        inc(S("sSL"), sy.dma_start(hv(QKP[96:128, :]), qp4[0][:, :, tsl1]), 16)
        inc(S("sSL2"), sc.dma_start(hv(KKR[s % 2][0:64, :]), k3[:, :, tsl0]), 16)
        inc(S("sSL2"), sc.dma_start(hv(KKR[s % 2][64:128, :]), k3[:, :, tsl1]), 16)
        inc(S("sSL2"), sc.dma_start(hv(KKP[0:32, :]), kp4[1][:, :, tsl0]), 16)
        inc(S("sSL2"), sc.dma_start(hv(KKP[32:64, :]), kp4[0][:, :, tsl0]), 16)
        inc(S("sSL2"), sc.dma_start(hv(KKP[64:96, :]), kp4[1][:, :, tsl1]), 16)
        inc(S("sSL2"), sc.dma_start(hv(KKP[96:128, :]), kp4[0][:, :, tsl1]), 16)
        inc(S("sSLg"), gp.dma_start(
            VS2[s % 2][0:32, :].rearrange("p (d u) -> p d u", d=D),
            v3[:, :, tsl0]), 16)
        inc(S("sSLg"), gp.dma_start(
            VS2[s % 2][32:64, :].rearrange("p (d u) -> p d u", d=D),
            v3[:, :, tsl1]), 16)
        inc(S("sSLg"), gp.dma_start(hv(CSF), cs4[:, :, s, :]), 16)
        inc(S("sSLg"), gp.dma_start(hv(SNF), sn4[:, :, s, :]), 16)

    def emit_rope(s):
        w(ve, S("sSL"), bsl + 16 * 6 * (s + 1))
        w(ve, S("sSL2"), bsl2 + 16 * 6 * (s + 1))
        w(ve, S("sSLg"), bslg + 16 * 4 * (s + 1))
        if s >= 2:
            w(ve, S("sGR"), bgr + NGR3 * (s - 1))
        ve.tensor_mul(QKR[:, :], QKR[:, :], CSF[:, :])
        ve.tensor_mul(QKP[:, :], QKP[:, :], SNF[:, :])
        ve.tensor_add(QPK4[s % 2][0:64, :, 0, :], QKRv[0:64], QKPv[0:64])
        ve.tensor_add(QPK4[s % 2][64:128, :, 1, :], QKRv[64:128], QKPv[64:128])
        ve.tensor_mul(KKR[s % 2][:, :], KKR[s % 2][:, :], CSF[:, :])
        ve.tensor_mul(KKP[:, :], KKP[:, :], SNF[:, :])
        i0 = ve.tensor_add(KKR[s % 2][:, :], KKR[s % 2][:, :], KKP[:, :])
        inc(S("sROPE"), i0)

    def emit_gram(s, g):
        gg = NGR3 * s + g
        w(te, S("sROPE"), brope + s + 1)
        if gg >= 2:
            w(te, S("sSUB"), bsub + gg - 1)
        for i in range(16):
            u = 16 * g + i
            mm = te.matmul(PS[gg % 2][0:64, i * 32:(i + 1) * 32],
                           QPK[s % 2][:, u * 64:(u + 1) * 64],
                           KKRv[s % 2][:, u, :], start=True, stop=True)
        inc(S("sGR"), mm)

    def emit_softmax(s, g):
        gg = NGR3 * s + g
        pg = PS[gg % 2][0:64, :].rearrange("p (u h) -> p u h", u=16)
        w(ve, S("sGR"), bgr + gg + 1)
        mx = MXZ[:, 16 * (gg % 2):16 * (gg % 2) + 16]
        ve.reduce_max(mx.unsqueeze(2), pg, axis=AX.X)
        if gg >= 2:
            w(ve, S("sEXP"), bexp + gg - 1)
        i1 = ve.tensor_tensor(DS[gg % 2].rearrange("p (u h) -> p u h", u=16),
                              pg, mx.unsqueeze(2).to_broadcast([64, 16, 32]),
                              ALU.subtract)
        inc(S("sSUB"), i1)
        if gg >= 2:
            w(sc, S("sSM"), bsm + gg - 1)
        w(sc, S("sSUB"), bsub + gg + 1)
        i2 = sc.activation(EB[gg % 2][:, :], DS[gg % 2][:, :], AF.Exp,
                           scale=SM_SCALE)
        inc(S("sEXP"), i2)
        w(ve, S("sEXP"), bexp + gg + 1)
        eb3 = EB[gg % 2].rearrange("p (u h) -> p u h", u=16)
        zr = MXZ[:, 32 + 16 * (gg % 2):48 + 16 * (gg % 2)]
        vsync(ve.reduce_sum(zr.unsqueeze(2), eb3, axis=AX.X))
        vsync(ve.reciprocal(zr, zr))
        ve.tensor_tensor(eb3, eb3, zr.unsqueeze(2).to_broadcast([64, 16, 32]),
                         ALU.mult)
        if gg >= 2:
            w(ve, S("sPV"), bpv + gg - 1)
        ve.transpose(ETt[gg % 2][:, :], EB[gg % 2][:, :])
        ve.tensor_copy(ET2v[gg % 2][0:32, :, 0, :], ETtv[gg % 2][0:32])
        i3 = ve.tensor_copy(ET2v[gg % 2][32:64, :, 1, :], ETtv[gg % 2][32:64])
        inc(S("sSM"), i3)

    def emit_pv(s, g):
        gg = NGR3 * s + g
        w(te, S("sSM"), bsm + gg + 1)
        w(te, S("sSLg"), bslg + 16 * 4 * (s + 1))
        if gg >= 2:
            w(te, S("sOEV"), boev + 2 * (gg - 1))
        for i in range(16):
            u = 16 * g + i
            bank = PS[2 + 2 * (g % 2) + i // 8]
            mm = te.matmul(bank[0:64, (i % 8) * 64:(i % 8) * 64 + 64],
                           VS2v[s % 2][:, :, u],
                           ET2[gg % 2][:, i * 64:(i + 1) * 64],
                           start=True, stop=True)
        inc(S("sPV"), mm)

    def emit_oev(s, g):
        gg = NGR3 * s + g
        w(sc, S("sPV"), bpv + gg + 1)
        if g == 0 and s >= 1:
            w(sc, S("sODMA"), bodma + 16 * s)
        for b in range(2):
            src = PS[2 + 2 * (g % 2) + b][0:64, :].rearrange(
                "p (u two h) -> p h two u", u=8, two=2)
            i0 = sc.copy(OSL4[:, :, :, g * 16 + b * 8: g * 16 + b * 8 + 8], src)
            inc(S("sOEV"), i0)

    emit_loads(0)
    emit_rope(0)
    for s in range(NSLAB):
        if s + 1 < NSLAB:
            emit_loads(s + 1)
        emit_gram(s, 0)
        emit_softmax(s, 0)
        emit_gram(s, 1)
        emit_softmax(s, 1)
        if s + 1 < NSLAB:
            emit_rope(s + 1)
        emit_pv(s, 0)
        emit_gram(s, 2)
        emit_softmax(s, 2)
        emit_pv(s, 1)
        emit_oev(s, 0)
        emit_gram(s, 3)
        emit_softmax(s, 3)
        emit_pv(s, 2)
        emit_oev(s, 1)
        emit_pv(s, 3)
        emit_oev(s, 2)
        emit_oev(s, 3)
        w(gp, S("sOEV"), boev + 8 * (s + 1))
        inc(S("sODMA"), gp.dma_start(
            o3[:, :, slice(s * TS, (s + 1) * TS)], OSL3[:, :, :]), 16)

    emit_tap("rot", QPK[1][:, :], ["sROPE"])
    emit_tap("eb", EB[1][:, :], ["sSM"])
    emit_tap("ds", DS[1][:, :], ["sSUB"])
    emit_tap("osl", OSL[:, :], ["sOEV"])
    join(ALL_ENG, ["sODMA", "sTAP"])
    stP3.close()
    emit_tap("os", o_scr[:, :], ["sODMA"])
    if stop == "P3":
        return finish()

    ph("P4_wo")
    stP4 = mkstack()
    OFM = sbuf("OFM", [128, NCH * T], BF16, stP4)
    ofm3 = OFM.rearrange("p (c t) -> p c t", c=NCH)
    osc3 = o_scr.rearrange("(c p) t -> p c t", p=128)
    w(sy, S("sODMA"))
    w(gp, S("sODMA"))
    inc(S("sQD"), sy.dma_start(ofm3[:, 0:NCH // 2], osc3[:, 0:NCH // 2]), 16)
    inc(S("sQD2"), gp.dma_start(ofm3[:, NCH // 2:], osc3[:, NCH // 2:]), 16)
    join((te, ve, sc, gp), ["sQD", "sQD2"])

    def ofm_chunk(k, nh):
        return OFM[:, k * T + nh * 512: k * T + (nh + 1) * 512]

    be4 = S("sEV").n

    def wo_consume(m, bp, be):
        w(ve, S("sPE"), bp + m + 1)
        ve.tensor_copy(achunk(m, 0), PS[2 * (m % 2)])
        i = ve.tensor_copy(achunk(m, 1), PS[2 * (m % 2) + 1])
        inc(S("sEV"), i)

    run_proj([wo_d[m] for m in range(NCH)], NCH, ofm_chunk, wo_consume)
    emit_ln_fields(afull, achunk, "sEV", be4, double=False)
    join(ALL_ENG, ["sFLD", "sEV"])
    stP4.close()
    emit_tap("ao", ACT[:, :], ["sFLD"])
    if stop == "P4":
        return finish()

    ph("P5_resid")
    stP5 = mkstack()
    Y32 = sbuf("Y32", [128, 1024], F32, stP5)
    for c in range(NCH):
        ve.tensor_mul(CM0[:, :], afull(c), FB0)
        ve.tensor_sub(Y32[:, :], CM0[:, :], FB1)
        i0 = ve.tensor_add(xchunk(c), xchunk(c), Y32[:, :])
    inc(S("sEV"), i0)
    join(ALL_ENG, ["sEV"])
    stP5.close()
    emit_tap("x1", X[:, :], ["sEV"])
    if stop == "P5":
        return finish()

    ph("P6_ffnLN")
    bc6 = S("sCV").n
    for c in range(NCH):
        inc(S("sCV"), ve.tensor_copy(afull(c), xchunk(c)))
    emit_ln_fields(afull, achunk, "sCV", bc6, double=False)
    join((ve,), ["sFLD"])
    for c in range(NCH):
        ve.tensor_mul(CM0[:, :], afull(c), FB0)
        i0 = ve.tensor_sub(afull(c), CM0[:, :], FB1)
    inc(S("sEV"), i0)
    join((sy,), ["sEV"])
    inc(S("sXSP"), sy.dma_start(x_spill[:, 0:8192], X[:, 0:8192]), 16)
    inc(S("sXSP"), sy.dma_start(x_spill[:, 8192:16384], X[:, 8192:16384]), 16)
    join(ALL_ENG, ["sXSP", "sEV"])
    emit_tap("f", ACT[:, :], ["sEV"])
    if stop == "P6":
        return finish()

    ph("P7_gate")
    MOE = BIGA

    def moechunk(c):
        return MOE[:, c * T:(c + 1) * T]

    stCMF = mkstack()
    CMF = sbuf("CMF", [16, 1024], BF16, stCMF)
    sel_sb = sbuf("sel_sb", [E, E * 128], BF16, stCMF)
    inc(S("sCAST"), gp.dma_start(sel_sb, sel_d), 16)
    stP7 = mkstack()
    GL = sbuf("GL", [16, 1024], F32, stP7)
    GLT = sbuf("GLT", [128, 128], F32, stP7)
    GLE = sbuf("GLE", [128, 128], F32, stP7)
    MX8 = sbuf("MX8", [128, 16], F32, stP7)
    MSK = sbuf("MSK", [128, 16], F32, stP7)
    CMBb = sbuf("CMBb", [128, 128], BF16, stP7)

    for k in range(NCH):
        te.matmul(PS[4][0:16, :], gate_sb[:, k * E:(k + 1) * E], achunk(k, 0),
                  start=(k == 0), stop=(k == NCH - 1))
        mm = te.matmul(PS[5][0:16, :], gate_sb[:, k * E:(k + 1) * E], achunk(k, 1),
                       start=(k == 0), stop=(k == NCH - 1))
    inc(S("sPE"), mm)
    w(ve, S("sPE"))
    ve.tensor_copy(GL[:, 0:512], PS[4][0:16, :])
    i0 = ve.tensor_copy(GL[:, 512:1024], PS[5][0:16, :])
    inc(S("sEV"), i0)
    w(te, S("sEV"))
    for j in range(NSLAB):
        mm = te.transpose(PS[6][:, j * 16:(j + 1) * 16],
                          GL[:, j * 128:(j + 1) * 128], ident[0:16, 0:16])
    inc(S("sPE"), mm)
    w(ve, S("sPE"))
    vsync(ve.tensor_copy(GLT[:, :], PS[6][:, 0:128]))
    g3 = GLT.rearrange("p (s e) -> p s e", s=NSLAB)
    mx8s = MX8[:, 0:8]
    vsync(ve.reduce_max(MX8[:, 8:16].unsqueeze(2), g3, axis=AX.X))
    i0 = ve.tensor_tensor(g3, g3, MX8[:, 8:16].unsqueeze(2).to_broadcast(
        [128, NSLAB, 16]), ALU.subtract)
    inc(S("sEV"), i0)
    w(sc, S("sEV"))
    inc(S("sEXP"), sc.activation(GLE[:, :], GLT[:, :], AF.Exp))
    w(ve, S("sEXP"))
    e3 = GLE.rearrange("p (s e) -> p s e", s=NSLAB)
    vsync(ve.reduce_sum(MX8[:, 8:16].unsqueeze(2), e3, axis=AX.X))
    vsync(ve.reciprocal(MX8[:, 8:16], MX8[:, 8:16]))
    vsync(ve.tensor_tensor(e3, e3, MX8[:, 8:16].unsqueeze(2).to_broadcast(
        [128, NSLAB, 16]), ALU.mult))
    for j in range(NSLAB):
        tile = GLE[:, j * 16:(j + 1) * 16]
        vsync(ve.max(out=mx8s, in_=tile))
        vsync(ve.tensor_scalar(MSK[:, :], tile, mx8s[:, 5:6], None, ALU.is_ge))
        i0 = ve.tensor_mul(CMBb[:, j * 16:(j + 1) * 16], MSK[:, :], tile)
    inc(S("sEV"), i0)
    emit_tap("logits", GLE[:, :], ["sEV"])
    w(te, S("sEV"))
    ps7b = PS[7].bitcast(BF16)
    for j in range(NSLAB):
        mm = te.transpose(ps7b[0:16, j * 128:(j + 1) * 128],
                          CMBb[:, j * 16:(j + 1) * 16], identb)
    inc(S("sPE"), mm)
    w(ve, S("sPE"))
    inc(S("sCMB"), ve.tensor_copy(CMF[:, :], ps7b[0:16, 0:1024]))
    join(ALL_ENG, ["sCMB"])
    emit_tap("cmb", CMF[:, :], ["sCMB"])
    stP7.close()
    if stop == "P7":
        return finish()

    ph("P8_shared")
    join((te, ve, sc, gp, sy), ["sXSP"])
    stP8 = mkstack()
    MID = sbuf("MID", [128, NKS * T], BF16, stP8)
    SG = [sbuf(f"SGs{i}", [128, 1024], BF16, stP8) for i in range(2)]

    def midchunk(k, nh=None):
        if nh is None:
            return MID[:, k * T:(k + 1) * T]
        return MID[:, k * T + nh * 512: k * T + (nh + 1) * 512]

    def run_up(blocks_g, blocks_u, nk, rhs_fn, consume_pair):
        npr = len(blocks_g)
        w(te, S("sEV"))
        w(gp, S("sPE"))
        wsem = (S("sW"), S("sW2"))
        bwp = (wsem[0].n, wsem[1].n)
        bp, be = S("sPE").n, S("sEV").n
        for b in range(2 * npr):
            blk = blocks_g[b // 2] if b % 2 == 0 else blocks_u[b // 2]
            if b >= 2:
                w(gp, S("sPE"), bp + b - 1)
            inc(wsem[b % 2], gp.dma_start(
                WB[b % 2][:, 0:nk * 128], blk, max_dma_last_dim=8192), 16)
        for j in range(npr):
            q = 4 * (j % 2)
            if j >= 2:
                w(te, S("sEV"), be + j - 1)
            for half in range(2):
                b = 2 * j + half
                w(te, wsem[b % 2], bwp[b % 2] + 16 * (b // 2 + 1))
                pq = q + 2 * half
                for k in range(nk):
                    lhs = WB[b % 2][:, k * 128:(k + 1) * 128]
                    te.matmul(PS[pq], lhs, rhs_fn(k, 0), start=(k == 0),
                              stop=(k == nk - 1))
                    mm = te.matmul(PS[pq + 1], lhs, rhs_fn(k, 1), start=(k == 0),
                                   stop=(k == nk - 1))
                inc(S("sPE"), mm)
        for j in range(npr):
            consume_pair(j, bp, be)

    bsg8 = S("sSG").n

    def sh_consume(j, bp, be):
        q = 4 * (j % 2)
        w(sc, S("sPE"), bp + 2 * j + 1)
        if j >= 2:
            w(sc, S("sEV"), be + j - 1)
        sc.activation(SG[j % 2][:, 0:512], PS[q], AF.Silu)
        i = sc.activation(SG[j % 2][:, 512:1024], PS[q + 1], AF.Silu)
        inc(S("sSG"), i)
        w(ve, S("sPE"), bp + 2 * j + 2)
        w(ve, S("sSG"), bsg8 + j + 1)
        ve.tensor_tensor(midchunk(j, 0), SG[j % 2][:, 0:512], PS[q + 2], ALU.mult)
        i = ve.tensor_tensor(midchunk(j, 1), SG[j % 2][:, 512:1024], PS[q + 3],
                             ALU.mult)
        inc(S("sEV"), i)

    run_up([sw1_d[j] for j in range(NKS)], [sw1_d[NKS + j] for j in range(NKS)],
           NCH, achunk, sh_consume)
    join(ALL_ENG, ["sEV"])
    emit_tap("mid", MID[:, :], ["sEV"])

    def sh_dn_consume(m, bp, be):
        w(ve, S("sPE"), bp + m + 1)
        ve.tensor_copy(moechunk(m)[:, 0:512], PS[2 * (m % 2)])
        i = ve.tensor_copy(moechunk(m)[:, 512:1024], PS[2 * (m % 2) + 1])
        inc(S("sEV"), i)

    run_proj([sw2_d[m] for m in range(NCH)], NKS, midchunk, sh_dn_consume)
    join(ALL_ENG, ["sEV"])
    stP8.close()
    if stop == "P8":
        return finish()

    ph("P9_experts")
    stP9 = mkstack()
    AE = [sbuf(f"AEs{i}", [128, NKE * T], BF16, stP9) for i in range(2)]
    SG9 = [sbuf(f"SG9s{i}", [128, 1024], BF16, stP9) for i in range(2)]
    CME = [sbuf(f"CMEs{i}", [128, 1024], BF16, stP9) for i in range(2)]

    join((te,), ["sCAST"])
    for ep in range(E // 2):
        e0, e1 = 2 * ep, 2 * ep + 1
        w(te, S("sCMB"))
        if ep >= 1:
            w(te, S("sEV"))
        for i_e, e in enumerate((e0, e1)):
            if i_e == 1:
                w(te, S("sCMB"))
            te.matmul(PS[6][:, :], sel_sb[:, e * 128:(e + 1) * 128],
                      CMF[:, 0:512], start=True, stop=True)
            mm = te.matmul(PS[7][:, :], sel_sb[:, e * 128:(e + 1) * 128],
                           CMF[:, 512:1024], start=True, stop=True)
            inc(S("sPE"), mm)
            w(sc, S("sPE"))
            sc.copy(CME[i_e][:, 0:512], PS[6][:, :])
            i0 = sc.copy(CME[i_e][:, 512:1024], PS[7][:, :])
            inc(S("sCMB"), i0)
        join((te, ve), ["sCMB"])

        for i_e, e in enumerate((e0, e1)):
            bsg = S("sSG").n

            def up_consume(j, bp, be, i_e=i_e, bsg=bsg):
                q = 4 * (j % 2)
                w(sc, S("sPE"), bp + 2 * j + 1)
                if j >= 2:
                    w(sc, S("sEV"), be + j - 1)
                sc.activation(SG9[j % 2][:, 0:512], PS[q], AF.Silu)
                i = sc.activation(SG9[j % 2][:, 512:1024], PS[q + 1], AF.Silu)
                inc(S("sSG"), i)
                w(ve, S("sPE"), bp + 2 * j + 2)
                w(ve, S("sSG"), bsg + j + 1)
                w(ve, S("sEV"), be + j)
                ve.tensor_tensor(CM0[:, 0:512], SG9[j % 2][:, 0:512], PS[q + 2],
                                 ALU.mult)
                ve.tensor_tensor(CM0[:, 512:1024], SG9[j % 2][:, 512:1024],
                                 PS[q + 3], ALU.mult)
                i = ve.tensor_mul(AE[i_e][:, j * T:(j + 1) * T], CM0[:, :],
                                  CME[i_e][:, :])
                inc(S("sEV"), i)

            run_up([w1_d[e, j] for j in range(NKE)],
                   [w1_d[e, NKE + j] for j in range(NKE)],
                   NCH, achunk, up_consume)

        w(te, S("sEV"))
        w(sy, S("sDN"))
        wdsem = (S("sWD"), S("sWD2"), S("sWD3"))
        bwdp = (wdsem[0].n, wdsem[1].n, wdsem[2].n)
        bdn = S("sDN").n
        bev2 = S("sEV2").n
        blocks = [(e_i, m) for m in range(NCH) for e_i in range(2)]
        for b, (e_i, m) in enumerate(blocks):
            if b >= 3:
                w(sy, S("sDN"), bdn + b - 2)
            e = (e0, e1)[e_i]
            inc(wdsem[b % 3], sy.dma_start(
                WD[b % 3][:, :], w2_d[e, m], max_dma_last_dim=8192), 16)
        for m in range(NCH):
            q = 2 * (m % 2)
            if m >= 2:
                w(te, S("sEV2"), bev2 + m - 1)
            for e_i in range(2):
                b = 2 * m + e_i
                w(te, wdsem[b % 3], bwdp[b % 3] + 16 * (b // 3 + 1))
                for k in range(NKE):
                    lhs = WD[b % 3][:, k * 128:(k + 1) * 128]
                    te.matmul(PS[q], lhs,
                              AE[e_i][:, k * T: k * T + 512],
                              start=(e_i == 0 and k == 0),
                              stop=(e_i == 1 and k == NKE - 1))
                    mm = te.matmul(PS[q + 1], lhs,
                                   AE[e_i][:, k * T + 512: (k + 1) * T],
                                   start=(e_i == 0 and k == 0),
                                   stop=(e_i == 1 and k == NKE - 1))
                inc(S("sDN"), mm)
        for m in range(NCH):
            w(ve, S("sDN"), bdn + 2 * (m + 1))
            q = 2 * (m % 2)
            ve.tensor_tensor(moechunk(m)[:, 0:512], moechunk(m)[:, 0:512],
                             PS[q], ALU.add)
            i = ve.tensor_tensor(moechunk(m)[:, 512:1024],
                                 moechunk(m)[:, 512:1024], PS[q + 1], ALU.add)
            inc(S("sEV2"), i)
        join((te,), ["sEV2"])
    join(ALL_ENG, ["sEV2"])
    stP9.close()
    emit_tap("moe", MOE[:, :], ["sEV2"])
    if stop == "P9":
        return finish()

    ph("P10_postLN")
    stP10 = mkstack()
    FF0 = sbuf("FF0", [128, 1024], F32, stP10)
    FF1 = sbuf("FF1", [128, 1024], F32, stP10)
    Y32b = sbuf("Y32b", [128, 1024], F32, stP10)

    bst10 = S("sST").n
    bc10 = S("sCV").n
    for c in range(NCH):
        if c >= 2:
            w(ve, S("sST"), bst10 + (c - 1))
        inc(S("sCV"), ve.tensor_copy(CM0 if c % 2 == 0 else CM1, moechunk(c)))

    def mchunk_full(c):
        return CM0 if c % 2 == 0 else CM1

    emit_ln_fields(lambda c: mchunk_full(c)[:, :],
                   lambda c, nh: mchunk_full(c)[:, nh * 512:(nh + 1) * 512],
                   "sCV", bc10, double=False, f32_fields=(FF0, FF1))
    join((ve,), ["sFLD"])
    for c in range(NCH):
        ve.tensor_mul(Y32b[:, :], moechunk(c), FF0)
        i0 = ve.tensor_sub(moechunk(c), Y32b[:, :], FF1)
    inc(S("sEV"), i0)
    join(ALL_ENG, ["sEV"])
    stP10.close()
    if stop == "P10":
        return finish()

    ph("P11_out")
    stP11 = mkstack()
    TMo = [sbuf(f"TMo{i}", [128, DM], F32, stP11) for i in range(2)]
    XSL = [sbuf(f"XSL{i}", [128, NCH * 128], F32, stP11) for i in range(2)]
    moe3 = MOE.rearrange("p (c t) -> p c t", c=NCH)
    xsp3 = x_spill.rearrange("p (c t) -> p c t", c=NCH)
    bpe11 = S("sPE").n
    bev11 = S("sEV").n
    qsem3 = (S("sXL"), S("sXL2"))
    bq11p = (qsem3[0].n, qsem3[1].n)
    osem = (S("sOUT"), S("sOUT2"))
    boutp = (osem[0].n, osem[1].n)
    badd = S("sEV2").n
    w(gp, S("sXSP"))
    for j in range(NSLAB):
        if j >= 2:
            w(gp, S("sPE"), bpe11 + j - 1)
        inc(qsem3[j % 2], gp.dma_start(
            XSL[j % 2].rearrange("p (c t) -> p c t", c=NCH),
            xsp3[:, :, j * 128:(j + 1) * 128]), 16)
    for j in range(NSLAB):
        w(ve, qsem3[j % 2], bq11p[j % 2] + 16 * (j // 2 + 1))
        if j >= 2:
            w(ve, S("sPE"), bpe11 + j - 1)
        i0 = ve.tensor_tensor(XSL[j % 2].rearrange("p (c t) -> p c t", c=NCH),
                              XSL[j % 2].rearrange("p (c t) -> p c t", c=NCH),
                              moe3[:, :, j * 128:(j + 1) * 128], ALU.add)
        inc(S("sEV2"), i0)
        w(te, S("sEV2"), badd + j + 1)
        if j >= 1:
            w(te, S("sEV"), bev11 + j)
        for c in range(NCH):
            mm = te.transpose(PS[4 + c % 4][:, (c // 4) * 128:(c // 4) * 128 + 128],
                              XSL[j % 2][:, c * 128:(c + 1) * 128], ident)
        inc(S("sPE"), mm)
        w(sc, S("sPE"), bpe11 + j + 1)
        if j >= 2:
            w(sc, osem[j % 2], boutp[j % 2] + 16 * (j // 2))
        for c in range(NCH):
            i0 = sc.copy(TMo[j % 2][:, c * 128:(c + 1) * 128],
                         PS[4 + c % 4][:, (c // 4) * 128:(c // 4) * 128 + 128])
        inc(S("sEV"), i0)
        oq = sy if j % 2 == 0 else sc
        w(oq, S("sEV"), bev11 + j + 1)
        inc(osem[j % 2],
            oq.dma_start(out_d[j * 128:(j + 1) * 128, :], TMo[j % 2]), 16)
    join(ALL_ENG, ["sOUT", "sOUT2"])
    stP11.close()
    stCMF.close()
    return finish()



def _mblocks(wT):
    K, M = wT.shape
    nk, nm = K // 128, M // 128
    b = wT.reshape(nk, 128, nm, 128).transpose(2, 1, 0, 3)
    return np.ascontiguousarray(b.reshape(nm, 128, K))


def _prep_inputs(inputs):
    f32 = np.float32
    hs = np.asarray(inputs["L0_attn_hidden_states"], f32).reshape(8192, DM)
    wq = np.asarray(inputs["L0_attn_w_q"], f32)
    wk = np.asarray(inputs["L0_attn_w_k"], f32)
    wv = np.asarray(inputs["L0_attn_w_v"], f32)
    wo = np.asarray(inputs["L0_attn_w_o"], f32)
    gamma = np.asarray(inputs["L0_attn_ln_gamma"], f32)
    beta = np.asarray(inputs["L0_attn_ln_beta"], f32)
    gate_w = np.asarray(inputs["L0_ffn_gate_w"], f32)
    w1 = np.asarray(inputs["L0_ffn_w1"], f32)
    w2 = np.asarray(inputs["L0_ffn_w2"], f32)
    sw1 = np.asarray(inputs["L0_ffn_shared_w1"], f32)
    sw2 = np.asarray(inputs["L0_ffn_shared_w2"], f32)

    com = {}
    com["wq_b"] = _mblocks(wq).astype(BF)
    com["wk_b"] = _mblocks(wk).astype(BF)
    com["wv_b"] = _mblocks(wv).astype(BF)
    com["wo_b"] = _mblocks(wo).astype(BF)
    com["gamma_pp"] = np.ascontiguousarray(gamma.reshape(NCH, 128).T)
    com["beta_pp"] = np.ascontiguousarray(beta.reshape(NCH, 128).T)
    com["gate_wT"] = np.ascontiguousarray(gate_w.T).astype(BF)
    com["sw1_b"] = _mblocks(np.ascontiguousarray(sw1.T)).astype(BF)
    com["sw2_b"] = _mblocks(np.ascontiguousarray(sw2.T)).astype(BF)
    com["w1_b"] = np.stack([_mblocks(np.ascontiguousarray(w1[e].T))
                            for e in range(E)]).astype(BF)
    com["w2_b"] = np.stack([_mblocks(np.ascontiguousarray(w2[e].T))
                            for e in range(E)]).astype(BF)
    com["ident"] = np.eye(128, dtype=f32)
    com["ones_in"] = np.ones((128, 128), dtype=f32)
    sel = np.zeros((E, E * 128), dtype=f32)
    for e in range(E):
        sel[e, e * 128:(e + 1) * 128] = 1.0
    com["sel"] = sel.astype(BF)

    inv_freq = (1.0 / (10000.0 ** (np.arange(0, D, 2, dtype=f32) / D))).astype(f32)
    ang = np.arange(8192, dtype=f32)[:, None] * inv_freq[None, :]
    cos_a = np.cos(ang).astype(f32)
    sin_a = np.sin(ang).astype(f32)
    sgn = np.where((np.arange(128) % 64) < 32, -1.0, 1.0).astype(f32)

    in_maps = []
    for i in range(NCORES):
        s0 = i * T
        m = dict(com)
        m["x_in"] = np.ascontiguousarray(hs[s0:s0 + T, :])
        NH = H * T // 2
        cc = cos_a[s0:s0 + T, :].T.reshape(H, T // 128, 2, 64)
        ss = sin_a[s0:s0 + T, :].T.reshape(H, T // 128, 2, 64)
        cosf = np.concatenate(
            [np.broadcast_to(cc[:, :, 0, :].reshape(1, NH), (64, NH)),
             np.broadcast_to(cc[:, :, 1, :].reshape(1, NH), (64, NH))], axis=0)
        sinf = np.concatenate(
            [np.broadcast_to(ss[:, :, 0, :].reshape(1, NH), (64, NH)),
             np.broadcast_to(ss[:, :, 1, :].reshape(1, NH), (64, NH))], axis=0)
        m["cosf"] = np.ascontiguousarray(cosf).astype(BF)
        m["sinf"] = (sgn[:, None] * sinf).astype(BF)
        in_maps.append(m)
    return in_maps


def kernel(**inputs):
    in_maps = _prep_inputs(inputs)
    nc = build_program()
    res = run_bass_kernel_spmd(nc, in_maps, core_ids=list(range(NCORES)))
    out = np.concatenate([r["out"] for r in res.results], axis=0)
    return out.reshape(1, 8192, DM).astype(np.float32)


# version 14
# speedup vs baseline: 1.0044x; 1.0044x over previous
import numpy as np
import ml_dtypes
from contextlib import ExitStack

import concourse.bass as bass
import concourse.mybir as mybir
from concourse.bass_utils import run_bass_kernel_spmd

F32 = mybir.dt.float32
BF16 = mybir.dt.bfloat16
AF = mybir.ActivationFunctionType
ALU = mybir.AluOpType
AX = mybir.AxisListType
BF = ml_dtypes.bfloat16

NCORES = 8
T = 1024
DM = 2048
NCH = DM // 128
H, D = 32, 64
E = 16
I_E = 1408
I_S = 2816
NKE = I_E // 128
NKS = I_S // 128
EPS = 1e-5
SM_SCALE = 0.125

TS = 128
NSLAB = T // TS
GTOK = 16
NGRP = TS // GTOK


class SemC:
    __slots__ = ("h", "n")

    def __init__(self, h):
        self.h = h
        self.n = 0


def build_program(taps=(), stop=None, p3_slabs=NSLAB, p3_mode='full'):
    nc = bass.Bass("TRN2", target_bir_lowering=False)
    taps = set(taps)

    def din(name, shape, dt=F32):
        return nc.dram_tensor(name, list(shape), dt, kind="ExternalInput").ap()

    x_in = din("x_in", (T, DM))
    wq_d = din("wq_b", (NCH, 128, DM), BF16)
    wk_d = din("wk_b", (NCH, 128, DM), BF16)
    wv_d = din("wv_b", (NCH, 128, DM), BF16)
    wo_d = din("wo_b", (NCH, 128, DM), BF16)
    gam_d = din("gamma_pp", (128, NCH))
    bet_d = din("beta_pp", (128, NCH))
    gate_d = din("gate_wT", (DM, E), BF16)
    sw1_d = din("sw1_b", (2 * NKS, 128, DM), BF16)
    sw2_d = din("sw2_b", (NCH, 128, I_S), BF16)
    w1_d = din("w1_b", (E, 2 * NKE, 128, DM), BF16)
    w2_d = din("w2_b", (E, NCH, 128, I_E), BF16)
    id_d = din("ident", (128, 128))
    ones_d = din("ones_in", (128, 128))
    sel_d = din("sel", (E, E * 128), BF16)
    cosf_d = din("cosf", (128, H * T // 2), BF16)
    sinf_d = din("sinf", (128, H * T // 2), BF16)

    out_d = nc.dram_tensor("out", [T, DM], F32, kind="ExternalOutput").ap()

    q_scr = nc.dram_tensor("q_scr", [DM, T], BF16, kind="Internal").ap()
    k_scr = nc.dram_tensor("k_scr", [DM, T], BF16, kind="Internal").ap()
    v_scr = nc.dram_tensor("v_scr", [DM, T], BF16, kind="Internal").ap()
    o_scr = nc.dram_tensor("o_scr", [DM, T], BF16, kind="Internal").ap()
    x_spill = nc.dram_tensor("x_spill", [128, NCH * T], F32, kind="Internal").ap()

    tap_t = {}
    tap_shapes = {
        "a": ((128, NCH * T), BF16), "f": ((128, NCH * T), BF16),
        "ao": ((128, NCH * T), BF16), "x1": ((128, NCH * T), F32),
        "mid": ((128, NKS * T), BF16), "moe": ((128, NCH * T), F32),
        "cmb": ((E, T), BF16), "qs": ((DM, T), BF16), "ks": ((DM, T), BF16),
        "vs": ((DM, T), BF16), "os": ((DM, T), BF16), "logits": ((128, 128), F32),
        "rot": ((128, 4096), BF16), "osl": ((64, H * TS), BF16),
        "eb": ((64, 512), BF16), "ds": ((64, 512), F32),
    }
    for name in taps:
        shp, dt = tap_shapes[name]
        tap_t[name] = nc.dram_tensor("tap_" + name, list(shp), dt,
                                     kind="ExternalOutput").ap()

    st = ExitStack()
    all_stacks = []

    def mkstack():
        s_ = ExitStack()
        all_stacks.append(s_)
        return s_

    def sbuf(name, shape, dt, stack=None):
        return (stack or st).enter_context(nc.sbuf_tensor(name, shape, dt)).ap()

    PS = [st.enter_context(nc.psum_tensor(f"psb{i}", [128, 512], F32)).ap()
          for i in range(8)]

    sems = {}
    for s in ("sW", "sPE", "sEV", "sEV2", "sSQ", "sST", "sFLD", "sCAST",
              "sSL", "sSLg", "sROPE", "sGR", "sSUB", "sEXP", "sSM", "sPV",
              "sOEV", "sODMA", "sQD", "sXSP", "sOUT", "sAUX", "sCMB",
              "sTAP", "sSG", "sDN", "sWD", "sVC", "sW2", "sQD2", "sWD2",
              "sOUT2", "sCV", "sXL", "sXL2", "sWD3", "sSL2", "sRK"):
        sems[s] = SemC(st.enter_context(nc.semaphore(s)))

    te, ve, sc, gp, sy = nc.tensor, nc.vector, nc.scalar, nc.gpsimd, nc.sync
    ALL_ENG = (te, ve, sc, gp, sy)

    def S(n):
        return sems[n]

    def inc(sm, inst, k=1):
        inst.then_inc(sm.h, k)
        sm.n += k
        return inst

    def w(eng, sm, tgt=None):
        eng.wait_ge(sm.h, sm.n if tgt is None else tgt)

    def join(engines, sem_names):
        for e_ in engines:
            for sn in sem_names:
                w(e_, S(sn))

    def esync(eng, inst):
        inc(S("sVC"), inst)
        w(eng, S("sVC"))
        return inst

    def vsync(inst):
        return esync(ve, inst)

    _scope = [None]

    def ph(name):
        if _scope[0] is not None:
            nc.leave_named_scope(_scope[0][0], _scope[0][1], False)
        _scope[0] = (name, nc.enter_named_scope(name, False)[0])

    ident = sbuf("identsb", [128, 128], F32)
    identb = sbuf("identb", [128, 128], BF16)
    ones_bf = sbuf("ones_bf", [128, 128], BF16)
    gam = sbuf("gam", [128, NCH], F32)
    bet = sbuf("bet", [128, NCH], F32)
    gate_sb = sbuf("gate_sb", [128, E * NCH], BF16)
    TNa = sbuf("TNa", [65, 1024], F32)
    TNb = sbuf("TNb", [65, 1024], F32)
    TNc = sbuf("TNc", [65, 1024], F32)
    TNbf = sbuf("TNbf", [1, 2048], BF16)
    FB0 = sbuf("FB0", [128, 1024], BF16)
    FB1 = sbuf("FB1", [128, 1024], BF16)
    CM0 = sbuf("CM0", [128, 1024], BF16)
    CM1 = sbuf("CM1", [128, 1024], BF16)
    SQ = [sbuf(f"SQst{i}", [128, 1024], BF16) for i in range(2)]
    ACT = sbuf("ACT", [128, NCH * T], BF16)
    WB = [sbuf(f"WB{i}", [128, NKS * 128], BF16) for i in range(2)]
    WD = [sbuf(f"WD{i}", [128, NKE * 128], BF16) for i in range(3)]

    def afull(c):
        return ACT[:, c * T:(c + 1) * T]

    def achunk(c, nh):
        return ACT[:, c * T + nh * 512: c * T + (nh + 1) * 512]

    def emit_ln_fields(chunk_full, chunk_half, ready, ready_base, double,
                       f32_fields=None, ready_step=1):
        bsq, bst = S("sSQ").n, S("sST").n
        for c in range(NCH):
            w(sc, S(ready), ready_base + (c + 1) * ready_step)
            if c >= 2:
                w(sc, S("sST"), bst + (c - 1))
            inc(S("sSQ"), sc.activation(SQ[c % 2][:, :], chunk_full(c), AF.Square))
        for c in range(NCH):
            w(te, S(ready), ready_base + (c + 1) * ready_step)
            w(te, S("sSQ"), bsq + c + 1)
            for nh in range(2):
                te.matmul(PS[4 + nh][0:1, :], ones_bf[:, 0:1], chunk_half(c, nh),
                          start=(c == 0), stop=(c == NCH - 1))
            for nh in range(2):
                mm = te.matmul(PS[4 + nh][32:33, :], ones_bf[:, 0:1],
                               SQ[c % 2][:, nh * 512:(nh + 1) * 512],
                               start=(c == 0), stop=(c == NCH - 1))
            inc(S("sST"), mm)
        w(ve, S("sST"))
        a0 = TNa[0:1, :]
        a32 = TNa[32:33, :]
        b0 = TNb[0:1, :]
        b32 = TNb[32:33, :]
        c0 = TNc[0:1, :]
        c32 = TNc[32:33, :]
        ve.tensor_copy(TNa[0:1, 0:512], PS[4][0:1, :])
        ve.tensor_copy(TNa[0:1, 512:1024], PS[5][0:1, :])
        ve.tensor_copy(TNa[32:33, 0:512], PS[4][32:33, :])
        ve.tensor_copy(TNa[32:33, 512:1024], PS[5][32:33, :])
        mu, cc = b0, c0
        ve.tensor_scalar(mu, a0, 1.0 / DM, None, ALU.mult)
        ve.tensor_scalar(c0, a32, 1.0 / DM, None, ALU.mult)
        ve.tensor_mul(a0, mu, mu)
        ve.tensor_sub(c0, c0, a0)
        i1 = ve.tensor_scalar(a0, c0, EPS, None, ALU.add)
        inc(S("sEV"), i1)
        w(sc, S("sEV"))
        if double:
            esync(sc, sc.activation(b32, a0, AF.Ln))
            sc.activation(a32, b32, AF.Exp, scale=-0.5)
            i2 = sc.activation(a0, b32, AF.Exp, scale=-1.0)
            inc(S("sEXP"), i2)
            w(ve, S("sEXP"))
            ve.tensor_mul(a0, c0, a0)
            i3 = ve.tensor_scalar(a0, a0, EPS, None, ALU.add)
            inc(S("sEV"), i3)
            w(sc, S("sEV"))
            esync(sc, sc.activation(b32, a0, AF.Ln))
            i4 = sc.activation(c32, b32, AF.Exp, scale=-0.5)
            inc(S("sEXP"), i4)
            w(ve, S("sEXP"))
            ve.tensor_mul(cc, a32, c32)
        else:
            esync(sc, sc.activation(b32, a0, AF.Ln))
            i2 = sc.activation(cc, b32, AF.Exp, scale=-0.5)
            inc(S("sEXP"), i2)
            w(ve, S("sEXP"))
        ve.tensor_mul(a0, mu, cc)
        ve.tensor_copy(TNbf[0:1, 0:1024], cc)
        i5 = ve.tensor_copy(TNbf[0:1, 1024:2048], a0)
        inc(S("sEV"), i5)
        w(te, S("sEV"))
        te.matmul(PS[6][:, :], ones_bf[0:1, :], TNbf[0:1, 0:512],
                  start=True, stop=True)
        te.matmul(PS[7][:, :], ones_bf[0:1, :], TNbf[0:1, 512:1024],
                  start=True, stop=True)
        te.matmul(PS[4][:, :], ones_bf[0:1, :], TNbf[0:1, 1024:1536],
                  start=True, stop=True)
        i6 = te.matmul(PS[5][:, :], ones_bf[0:1, :], TNbf[0:1, 1536:2048],
                       start=True, stop=True)
        inc(S("sST"), i6)
        w(ve, S("sST"))
        ve.tensor_copy(FB0[:, 0:512], PS[6][:, :])
        ve.tensor_copy(FB0[:, 512:1024], PS[7][:, :])
        ve.tensor_copy(FB1[:, 0:512], PS[4][:, :])
        i7 = ve.tensor_copy(FB1[:, 512:1024], PS[5][:, :])
        if f32_fields is not None:
            FF0, FF1 = f32_fields
            ve.tensor_copy(FF0[:, 0:512], PS[6][:, :])
            ve.tensor_copy(FF0[:, 512:1024], PS[7][:, :])
            ve.tensor_copy(FF1[:, 0:512], PS[4][:, :])
            i7 = ve.tensor_copy(FF1[:, 512:1024], PS[5][:, :])
        inc(S("sFLD"), i7)

    def run_proj(blocks_dram, nk, rhs_fn, consume, wbufs=None, dma_eng=None):
        wbufs = wbufs or WB
        dma_eng = dma_eng or gp
        nblk = len(blocks_dram)
        w(te, S("sEV"))
        w(dma_eng, S("sPE"))
        wsem = (S("sW"), S("sW2"))
        bwp = (wsem[0].n, wsem[1].n)
        bp, be = S("sPE").n, S("sEV").n
        for m in range(nblk):
            if m >= 2:
                w(dma_eng, S("sPE"), bp + m - 1)
            inc(wsem[m % 2], dma_eng.dma_start(
                wbufs[m % 2][:, 0:nk * 128], blocks_dram[m],
                max_dma_last_dim=8192), 16)
        for m in range(nblk):
            w(te, wsem[m % 2], bwp[m % 2] + 16 * (m // 2 + 1))
            if m >= 2:
                w(te, S("sEV"), be + m - 1)
            pA, pB = PS[2 * (m % 2)], PS[2 * (m % 2) + 1]
            for k in range(nk):
                lhs = wbufs[m % 2][:, k * 128:(k + 1) * 128]
                te.matmul(pA, lhs, rhs_fn(k, 0), start=(k == 0), stop=(k == nk - 1))
                mm = te.matmul(pB, lhs, rhs_fn(k, 1), start=(k == 0),
                               stop=(k == nk - 1))
            inc(S("sPE"), mm)
        for m in range(nblk):
            consume(m, bp, be)

    def emit_tap(name, src_ap, wait_sems):
        if name not in tap_t:
            return
        for sn in wait_sems:
            w(sy, S(sn))
        inc(S("sTAP"), sy.dma_start(tap_t[name][:, :], src_ap), 16)

    def finish():
        if _scope[0] is not None:
            nc.leave_named_scope(_scope[0][0], _scope[0][1], False)
            _scope[0] = None
        join((sy,), ["sOUT", "sOUT2", "sTAP"])
        join((te, ve, sc, gp), ["sOUT", "sOUT2"])
        for stk in reversed(all_stacks):
            stk.close()
        st.close()
        return nc

    BIGA = sbuf("BIGA", [128, NCH * T], F32)
    X = BIGA

    def xchunk(c):
        return X[:, c * T:(c + 1) * T]

    ph("P0_in")
    stP0 = mkstack()
    TM = [sbuf(f"TM{i}", [128, DM], F32, stP0) for i in range(2)]

    inc(S("sAUX"), sy.dma_start(ident, id_d), 16)
    inc(S("sAUX"), sy.dma_start(gam, gam_d), 16)
    inc(S("sAUX"), sy.dma_start(bet, bet_d), 16)
    inc(S("sCAST"), gp.dma_start(identb, id_d), 16)
    inc(S("sCAST"), gp.dma_start(ones_bf, ones_d), 16)
    inc(S("sCAST"), gp.dma_start(
        gate_sb.rearrange("p (k e) -> p k e", k=NCH),
        gate_d.rearrange("(k p) e -> p k e", p=128)), 16)

    qsem = (S("sQD"), S("sQD2"))
    bq0p = (qsem[0].n, qsem[1].n)
    bp0, be0 = S("sPE").n, S("sEV").n
    for j in range(NSLAB):
        if j >= 2:
            w(sy, S("sPE"), bp0 + j - 1)
        inc(qsem[j % 2], sy.dma_start(TM[j % 2], x_in[j * 128:(j + 1) * 128, :]), 16)
    for j in range(NSLAB):
        w(te, qsem[j % 2], bq0p[j % 2] + 16 * (j // 2 + 1))
        if j == 0:
            w(te, S("sAUX"))
        if j >= 1:
            w(te, S("sEV"), be0 + j)
        for c in range(NCH):
            mm = te.transpose(PS[4 + c % 4][:, (c // 4) * 128:(c // 4) * 128 + 128],
                              TM[j % 2][:, c * 128:(c + 1) * 128], ident)
        inc(S("sPE"), mm)
        w(ve, S("sPE"), bp0 + j + 1)
        for c in range(NCH):
            i0 = ve.tensor_copy(xchunk(c)[:, j * 128:(j + 1) * 128],
                                PS[4 + c % 4][:, (c // 4) * 128:(c // 4) * 128 + 128])
        inc(S("sEV"), i0)
    join(ALL_ENG, ["sEV", "sQD", "sQD2", "sAUX", "sCAST"])
    stP0.close()
    if stop == "P0":
        return finish()

    ph("P1_preLN")
    bc1 = S("sCV").n
    for c in range(NCH):
        inc(S("sCV"), ve.tensor_copy(afull(c), xchunk(c)))
    emit_ln_fields(afull, achunk, "sCV", bc1, double=True)
    join((ve,), ["sFLD"])
    for c in range(NCH):
        ve.tensor_mul(CM0[:, :], afull(c), FB0)
        ve.tensor_sub(CM0[:, :], CM0[:, :], FB1)
        i0 = ve.tensor_scalar(afull(c), CM0[:, :], gam[:, c:c + 1],
                              bet[:, c:c + 1], ALU.mult, ALU.add)
    inc(S("sEV"), i0)
    join(ALL_ENG, ["sEV"])
    emit_tap("a", ACT[:, :], ["sEV"])
    if stop == "P1":
        return finish()

    ph("P2_qkv")
    stP2 = mkstack()
    QS = [sbuf(f"QSst{i}", [128, 1024], BF16, stP2) for i in range(2)]

    qk_mark = None
    for pj, (wmat, scr) in enumerate(((wq_d, q_scr), (wk_d, k_scr),
                                      (wv_d, v_scr))):
        qsem2 = (S("sQD"), S("sQD2"))
        bqdp = (qsem2[0].n, qsem2[1].n)

        def consume(m, bp, be, scr=scr, qsem2=qsem2, bqdp=bqdp):
            w(ve, S("sPE"), bp + m + 1)
            w(ve, qsem2[m % 2], bqdp[m % 2] + 16 * (m // 2))
            ve.tensor_copy(QS[m % 2][:, 0:512], PS[2 * (m % 2)])
            i = ve.tensor_copy(QS[m % 2][:, 512:1024], PS[2 * (m % 2) + 1])
            inc(S("sEV"), i)
            w(sy, S("sEV"))
            inc(qsem2[m % 2],
                sy.dma_start(scr[m * 128:(m + 1) * 128, :], QS[m % 2]), 16)

        run_proj([wmat[m] for m in range(NCH)], NCH, achunk, consume)
        if pj == 1:
            qk_mark = (S("sQD").n, S("sQD2").n)
    pe_full = S("sPE").n
    emit_tap("qs", q_scr[:, :], ["sQD", "sQD2"])
    emit_tap("ks", k_scr[:, :], ["sQD", "sQD2"])
    emit_tap("vs", v_scr[:, :], ["sQD", "sQD2"])
    if stop == "P2":
        return finish()

    ph("P3_attn")
    NGR3 = 4
    QKR = ACT[:, 0:2048]
    QKP = ACT[:, 2048:4096]
    KKP = ACT[:, 4096:6144]
    CSF = ACT[:, 6144:8192]
    SNF = ACT[:, 8192:10240]
    KKR = [ACT[:, 10240 + 2048 * i: 12288 + 2048 * i] for i in range(2)]

    q3 = q_scr.rearrange("(h d) t -> d h t", d=D)
    k3 = k_scr.rearrange("(h d) t -> d h t", d=D)
    qp4 = q_scr.rearrange("(h d2 p2) t -> p2 d2 h t", d2=32, p2=2)
    kp4 = k_scr.rearrange("(h d2 p2) t -> p2 d2 h t", d2=32, p2=2)
    v3 = v_scr.rearrange("(h d) t -> h d t", d=D)
    o3 = o_scr.rearrange("(h d) t -> d h t", d=D)
    cs4 = cosf_d.rearrange("p (h s u) -> p h s u", h=H, s=NSLAB)
    sn4 = sinf_d.rearrange("p (h s u) -> p h s u", h=H, s=NSLAB)

    bsl = S("sSL").n
    bsl2 = S("sSL2").n
    bslg = S("sSLg").n
    brope = S("sROPE").n
    brk = S("sRK").n
    bgr = S("sGR").n
    bexp = S("sEXP").n
    bsm = S("sSM").n
    bpv = S("sPV").n
    boev = S("sOEV").n
    bodma = S("sODMA").n

    def hv(ap):
        return ap.rearrange("p (h u) -> p h u", h=H)

    def emit_loads_syc(s):
        tsl0 = slice(s * TS, s * TS + 64)
        tsl1 = slice(s * TS + 64, (s + 1) * TS)
        if s >= 1:
            w(sy, S("sROPE"), brope + s)
            w(sy, S("sRK"), brk + s)
            w(sc, S("sROPE"), brope + s)
        if s >= 2:
            w(sc, S("sGR"), bgr + NGR3 * (s - 1))
        inc(S("sSL"), sy.dma_start(hv(QKR[0:64, :]), q3[:, :, tsl0]), 16)
        inc(S("sSL"), sy.dma_start(hv(QKR[64:128, :]), q3[:, :, tsl1]), 16)
        inc(S("sSL"), sy.dma_start(hv(KKP[0:32, :]), kp4[1][:, :, tsl0]), 16)
        inc(S("sSL"), sy.dma_start(hv(KKP[32:64, :]), kp4[0][:, :, tsl0]), 16)
        inc(S("sSL"), sy.dma_start(hv(KKP[64:96, :]), kp4[1][:, :, tsl1]), 16)
        inc(S("sSL"), sy.dma_start(hv(KKP[96:128, :]), kp4[0][:, :, tsl1]), 16)
        inc(S("sSL2"), sc.dma_start(hv(KKR[s % 2][0:64, :]), k3[:, :, tsl0]), 16)
        inc(S("sSL2"), sc.dma_start(hv(KKR[s % 2][64:128, :]), k3[:, :, tsl1]), 16)
        inc(S("sSL2"), sc.dma_start(hv(QKP[0:32, :]), qp4[1][:, :, tsl0]), 16)
        inc(S("sSL2"), sc.dma_start(hv(QKP[32:64, :]), qp4[0][:, :, tsl0]), 16)
        inc(S("sSL2"), sc.dma_start(hv(QKP[64:96, :]), qp4[1][:, :, tsl1]), 16)
        inc(S("sSL2"), sc.dma_start(hv(QKP[96:128, :]), qp4[0][:, :, tsl1]), 16)

    w(sy, S("sPE"), pe_full)
    w(sy, S("sQD"), qk_mark[0])
    w(sy, S("sQD2"), qk_mark[1])
    w(sc, S("sPE"), pe_full)
    w(sc, S("sQD"), qk_mark[0])
    w(sc, S("sQD2"), qk_mark[1])
    emit_loads_syc(0)
    join(ALL_ENG, ["sQD", "sQD2", "sEV"])
    stP2.close()
    emit_tap("qs", q_scr[:, :], ["sQD", "sQD2"])
    emit_tap("ks", k_scr[:, :], ["sQD", "sQD2"])
    emit_tap("vs", v_scr[:, :], ["sQD", "sQD2"])

    stP3 = mkstack()
    QPK = [sbuf(f"QPKs{i}", [128, 4096], BF16, stP3) for i in range(2)]
    VS2 = [sbuf(f"VS2s{i}", [64, 4096], BF16, stP3) for i in range(2)]
    OSL = sbuf("OSL", [64, H * TS], BF16, stP3)
    ETt = [sbuf(f"ETts{i}", [64, 512], BF16, stP3) for i in range(2)]
    ET2 = [sbuf(f"ET2s{i}", [64, 1024], BF16, stP3) for i in range(2)]
    EB = [sbuf(f"EBs{i}", [64, 512], BF16, stP3) for i in range(2)]
    MXZ = sbuf("MXZ", [64, 64], F32, stP3)

    QKRv = QKR.rearrange("p (h u) -> p u h", h=H)
    QKPv = QKP.rearrange("p (h u) -> p u h", h=H)
    QPK4 = [QPK[i].rearrange("p (u two h) -> p u two h", two=2, h=H)
            for i in range(2)]
    KKRv = [KKR[i].rearrange("p (h u) -> p u h", h=H) for i in range(2)]
    VS2v = [VS2[i].rearrange("p (d u) -> p d u", d=D) for i in range(2)]
    ET2v = [ET2[i].rearrange("p (u two h) -> p u two h", two=2, h=H)
            for i in range(2)]
    ETtv = [ETt[i].rearrange("p (u h) -> p u h", h=H) for i in range(2)]
    OSL3 = OSL.rearrange("p (h t) -> p h t", h=H)
    OSL4 = OSL.rearrange("p (h two u) -> p h two u", h=H, two=2)

    for i in range(2):
        ve.memzero(QPK4[i][64:128, :, 0, :])
        ve.memzero(QPK4[i][0:64, :, 1, :])
        ve.memzero(ET2v[i][32:64, :, 0, :])
        i0 = ve.memzero(ET2v[i][0:32, :, 1, :])
    vsync(i0)

    def emit_loads_gp(s):
        tsl0 = slice(s * TS, s * TS + 64)
        tsl1 = slice(s * TS + 64, (s + 1) * TS)
        if s >= 1:
            w(gp, S("sROPE"), brope + s)
        if s >= 2:
            w(gp, S("sPV"), bpv + NGR3 * (s - 1))
        inc(S("sSLg"), gp.dma_start(
            VS2[s % 2][0:32, :].rearrange("p (d u) -> p d u", d=D),
            v3[:, :, tsl0]), 16)
        inc(S("sSLg"), gp.dma_start(
            VS2[s % 2][32:64, :].rearrange("p (d u) -> p d u", d=D),
            v3[:, :, tsl1]), 16)
        inc(S("sSLg"), gp.dma_start(hv(CSF), cs4[:, :, s, :]), 16)
        inc(S("sSLg"), gp.dma_start(hv(SNF), sn4[:, :, s, :]), 16)

    def emit_rope_q(s):
        w(ve, S("sSL"), bsl + 16 * (6 * s + 2))
        w(ve, S("sSL2"), bsl2 + 16 * 6 * (s + 1))
        w(ve, S("sSLg"), bslg + 16 * 4 * (s + 1))
        if s >= 2:
            w(ve, S("sGR"), bgr + NGR3 * (s - 1))
        ve.tensor_mul(QKR[:, :], QKR[:, :], CSF[:, :])
        ve.tensor_mul(QKP[:, :], QKP[:, :], SNF[:, :])
        ve.tensor_add(QPK4[s % 2][0:64, :, 0, :], QKRv[0:64], QKPv[0:64])
        i0 = ve.tensor_add(QPK4[s % 2][64:128, :, 1, :], QKRv[64:128],
                           QKPv[64:128])
        inc(S("sROPE"), i0)

    def emit_rope_k(s):
        w(gp, S("sSL2"), bsl2 + 16 * (6 * s + 2))
        w(gp, S("sSL"), bsl + 16 * 6 * (s + 1))
        w(gp, S("sSLg"), bslg + 16 * 4 * (s + 1))
        if s >= 2:
            w(gp, S("sGR"), bgr + NGR3 * (s - 1))
        gp.tensor_mul(KKR[s % 2][:, :], KKR[s % 2][:, :], CSF[:, :])
        gp.tensor_mul(KKP[:, :], KKP[:, :], SNF[:, :])
        i0 = gp.tensor_add(KKR[s % 2][:, :], KKR[s % 2][:, :], KKP[:, :])
        inc(S("sRK"), i0)

    def emit_gram(s, g):
        gg = NGR3 * s + g
        w(te, S("sROPE"), brope + s + 1)
        w(te, S("sRK"), brk + s + 1)
        if gg >= 2:
            w(te, S("sEXP"), bexp + gg - 1)
        for i in range(16):
            u = 16 * g + i
            mm = te.matmul(PS[gg % 2][0:64, i * 32:(i + 1) * 32],
                           QPK[s % 2][:, u * 64:(u + 1) * 64],
                           KKRv[s % 2][:, u, :], start=True, stop=True)
        inc(S("sGR"), mm)

    def emit_softmax(s, g):
        gg = NGR3 * s + g
        w(sc, S("sGR"), bgr + gg + 1)
        if gg >= 2:
            w(sc, S("sSM"), bsm + gg - 1)
        i2 = sc.activation(EB[gg % 2][:, :], PS[gg % 2][0:64, :], AF.Exp,
                           scale=SM_SCALE)
        inc(S("sEXP"), i2)
        w(ve, S("sEXP"), bexp + gg + 1)
        eb3 = EB[gg % 2].rearrange("p (u h) -> p u h", u=16)
        zr = MXZ[:, 32 + 16 * (gg % 2):48 + 16 * (gg % 2)]
        vsync(ve.reduce_sum(zr.unsqueeze(2), eb3, axis=AX.X))
        vsync(ve.reciprocal(zr, zr))
        ve.tensor_tensor(eb3, eb3, zr.unsqueeze(2).to_broadcast([64, 16, 32]),
                         ALU.mult)
        if gg >= 2:
            w(ve, S("sPV"), bpv + gg - 1)
        ve.transpose(ETt[gg % 2][:, :], EB[gg % 2][:, :])
        ve.tensor_copy(ET2v[gg % 2][0:32, :, 0, :], ETtv[gg % 2][0:32])
        i3 = ve.tensor_copy(ET2v[gg % 2][32:64, :, 1, :], ETtv[gg % 2][32:64])
        inc(S("sSM"), i3)

    def emit_pv(s, g):
        gg = NGR3 * s + g
        w(te, S("sSM"), bsm + gg + 1)
        w(te, S("sSLg"), bslg + 16 * 4 * (s + 1))
        if gg >= 2:
            w(te, S("sOEV"), boev + 2 * (gg - 1))
        for i in range(16):
            u = 16 * g + i
            bank = PS[2 + 2 * (g % 2) + i // 8]
            mm = te.matmul(bank[0:64, (i % 8) * 64:(i % 8) * 64 + 64],
                           VS2v[s % 2][:, :, u],
                           ET2[gg % 2][:, i * 64:(i + 1) * 64],
                           start=True, stop=True)
        inc(S("sPV"), mm)

    def emit_oev(s, g):
        gg = NGR3 * s + g
        w(sc, S("sPV"), bpv + gg + 1)
        if g == 0 and s >= 1:
            w(sc, S("sODMA"), bodma + 16 * s)
        for b in range(2):
            src = PS[2 + 2 * (g % 2) + b][0:64, :].rearrange(
                "p (u two h) -> p h two u", u=8, two=2)
            i0 = sc.copy(OSL4[:, :, :, g * 16 + b * 8: g * 16 + b * 8 + 8], src)
            inc(S("sOEV"), i0)

    emit_loads_gp(0)
    emit_rope_k(0)
    emit_rope_q(0)
    for s in range(NSLAB):
        if s + 1 < NSLAB:
            emit_loads_syc(s + 1)
            emit_loads_gp(s + 1)
            emit_rope_k(s + 1)
        emit_gram(s, 0)
        emit_softmax(s, 0)
        emit_gram(s, 1)
        emit_softmax(s, 1)
        if s + 1 < NSLAB:
            emit_rope_q(s + 1)
        emit_pv(s, 0)
        emit_gram(s, 2)
        emit_softmax(s, 2)
        emit_pv(s, 1)
        emit_oev(s, 0)
        emit_gram(s, 3)
        emit_softmax(s, 3)
        emit_pv(s, 2)
        emit_oev(s, 1)
        emit_pv(s, 3)
        emit_oev(s, 2)
        emit_oev(s, 3)
        w(gp, S("sOEV"), boev + 8 * (s + 1))
        inc(S("sODMA"), gp.dma_start(
            o3[:, :, slice(s * TS, (s + 1) * TS)], OSL3[:, :, :]), 16)

    emit_tap("rot", QPK[1][:, :], ["sROPE"])
    emit_tap("eb", EB[1][:, :], ["sSM"])
    emit_tap("osl", OSL[:, :], ["sOEV"])
    join(ALL_ENG, ["sODMA", "sTAP"])
    stP3.close()
    emit_tap("os", o_scr[:, :], ["sODMA"])
    if stop == "P3":
        return finish()

    ph("P4_wo")
    stP4 = mkstack()
    OFM = sbuf("OFM", [128, NCH * T], BF16, stP4)
    ofm3 = OFM.rearrange("p (c t) -> p c t", c=NCH)
    osc3 = o_scr.rearrange("(c p) t -> p c t", p=128)
    w(sy, S("sODMA"))
    w(gp, S("sODMA"))
    w(sc, S("sODMA"))
    inc(S("sQD"), sy.dma_start(ofm3[:, 0:6], osc3[:, 0:6]), 16)
    inc(S("sQD2"), gp.dma_start(ofm3[:, 6:11], osc3[:, 6:11]), 16)
    inc(S("sQD"), sc.dma_start(ofm3[:, 11:16], osc3[:, 11:16]), 16)
    join((te, ve, sc, gp), ["sQD", "sQD2"])

    def ofm_chunk(k, nh):
        return OFM[:, k * T + nh * 512: k * T + (nh + 1) * 512]

    be4 = S("sEV").n

    def wo_consume(m, bp, be):
        w(ve, S("sPE"), bp + m + 1)
        ve.tensor_copy(achunk(m, 0), PS[2 * (m % 2)])
        i = ve.tensor_copy(achunk(m, 1), PS[2 * (m % 2) + 1])
        inc(S("sEV"), i)

    run_proj([wo_d[m] for m in range(NCH)], NCH, ofm_chunk, wo_consume)
    emit_ln_fields(afull, achunk, "sEV", be4, double=False)
    join(ALL_ENG, ["sFLD", "sEV"])
    stP4.close()
    emit_tap("ao", ACT[:, :], ["sFLD"])
    if stop == "P4":
        return finish()

    ph("P5_resid")
    stP5 = mkstack()
    Y32 = sbuf("Y32", [128, 1024], F32, stP5)
    for c in range(NCH):
        ve.tensor_mul(CM0[:, :], afull(c), FB0)
        ve.tensor_sub(Y32[:, :], CM0[:, :], FB1)
        i0 = ve.tensor_add(xchunk(c), xchunk(c), Y32[:, :])
    inc(S("sEV"), i0)
    join(ALL_ENG, ["sEV"])
    stP5.close()
    emit_tap("x1", X[:, :], ["sEV"])
    if stop == "P5":
        return finish()

    ph("P6_ffnLN")
    bc6 = S("sCV").n
    for c in range(NCH):
        inc(S("sCV"), ve.tensor_copy(afull(c), xchunk(c)))
    emit_ln_fields(afull, achunk, "sCV", bc6, double=False)
    join((ve,), ["sFLD"])
    for c in range(NCH):
        ve.tensor_mul(CM0[:, :], afull(c), FB0)
        i0 = ve.tensor_sub(afull(c), CM0[:, :], FB1)
    inc(S("sEV"), i0)
    join((sy,), ["sEV"])
    inc(S("sXSP"), sy.dma_start(x_spill[:, 0:8192], X[:, 0:8192]), 16)
    inc(S("sXSP"), sy.dma_start(x_spill[:, 8192:16384], X[:, 8192:16384]), 16)
    join(ALL_ENG, ["sXSP", "sEV"])
    emit_tap("f", ACT[:, :], ["sEV"])
    if stop == "P6":
        return finish()

    ph("P7_gate")
    MOE = BIGA

    def moechunk(c):
        return MOE[:, c * T:(c + 1) * T]

    stCMF = mkstack()
    CMF = sbuf("CMF", [16, 1024], BF16, stCMF)
    sel_sb = sbuf("sel_sb", [E, E * 128], BF16, stCMF)
    inc(S("sCAST"), gp.dma_start(sel_sb, sel_d), 16)
    stP7 = mkstack()
    GL = sbuf("GL", [16, 1024], F32, stP7)
    GLT = sbuf("GLT", [128, 128], F32, stP7)
    GLE = sbuf("GLE", [128, 128], F32, stP7)
    MX8 = sbuf("MX8", [128, 16], F32, stP7)
    MSK = sbuf("MSK", [128, 16], F32, stP7)
    CMBb = sbuf("CMBb", [128, 128], BF16, stP7)

    for k in range(NCH):
        te.matmul(PS[4][0:16, :], gate_sb[:, k * E:(k + 1) * E], achunk(k, 0),
                  start=(k == 0), stop=(k == NCH - 1))
        mm = te.matmul(PS[5][0:16, :], gate_sb[:, k * E:(k + 1) * E], achunk(k, 1),
                       start=(k == 0), stop=(k == NCH - 1))
    inc(S("sPE"), mm)
    w(ve, S("sPE"))
    ve.tensor_copy(GL[:, 0:512], PS[4][0:16, :])
    i0 = ve.tensor_copy(GL[:, 512:1024], PS[5][0:16, :])
    inc(S("sEV"), i0)
    w(te, S("sEV"))
    for j in range(NSLAB):
        mm = te.transpose(PS[6][:, j * 16:(j + 1) * 16],
                          GL[:, j * 128:(j + 1) * 128], ident[0:16, 0:16])
    inc(S("sPE"), mm)
    w(ve, S("sPE"))
    vsync(ve.tensor_copy(GLT[:, :], PS[6][:, 0:128]))
    g3 = GLT.rearrange("p (s e) -> p s e", s=NSLAB)
    mx8s = MX8[:, 0:8]
    vsync(ve.reduce_max(MX8[:, 8:16].unsqueeze(2), g3, axis=AX.X))
    i0 = ve.tensor_tensor(g3, g3, MX8[:, 8:16].unsqueeze(2).to_broadcast(
        [128, NSLAB, 16]), ALU.subtract)
    inc(S("sEV"), i0)
    w(sc, S("sEV"))
    inc(S("sEXP"), sc.activation(GLE[:, :], GLT[:, :], AF.Exp))
    w(ve, S("sEXP"))
    e3 = GLE.rearrange("p (s e) -> p s e", s=NSLAB)
    vsync(ve.reduce_sum(MX8[:, 8:16].unsqueeze(2), e3, axis=AX.X))
    vsync(ve.reciprocal(MX8[:, 8:16], MX8[:, 8:16]))
    vsync(ve.tensor_tensor(e3, e3, MX8[:, 8:16].unsqueeze(2).to_broadcast(
        [128, NSLAB, 16]), ALU.mult))
    for j in range(NSLAB):
        tile = GLE[:, j * 16:(j + 1) * 16]
        vsync(ve.max(out=mx8s, in_=tile))
        vsync(ve.tensor_scalar(MSK[:, :], tile, mx8s[:, 5:6], None, ALU.is_ge))
        i0 = ve.tensor_mul(CMBb[:, j * 16:(j + 1) * 16], MSK[:, :], tile)
    inc(S("sEV"), i0)
    emit_tap("logits", GLE[:, :], ["sEV"])
    w(te, S("sEV"))
    ps7b = PS[7].bitcast(BF16)
    for j in range(NSLAB):
        mm = te.transpose(ps7b[0:16, j * 128:(j + 1) * 128],
                          CMBb[:, j * 16:(j + 1) * 16], identb)
    inc(S("sPE"), mm)
    w(ve, S("sPE"))
    inc(S("sCMB"), ve.tensor_copy(CMF[:, :], ps7b[0:16, 0:1024]))
    join(ALL_ENG, ["sCMB"])
    emit_tap("cmb", CMF[:, :], ["sCMB"])
    stP7.close()
    if stop == "P7":
        return finish()

    ph("P8_shared")
    join((te, ve, sc, gp, sy), ["sXSP"])
    stP8 = mkstack()
    MID = sbuf("MID", [128, NKS * T], BF16, stP8)
    SG = [sbuf(f"SGs{i}", [128, 1024], BF16, stP8) for i in range(2)]

    def midchunk(k, nh=None):
        if nh is None:
            return MID[:, k * T:(k + 1) * T]
        return MID[:, k * T + nh * 512: k * T + (nh + 1) * 512]

    def run_up(blocks_g, blocks_u, nk, rhs_fn, consume_pair):
        npr = len(blocks_g)
        w(te, S("sEV"))
        w(gp, S("sPE"))
        wsem = (S("sW"), S("sW2"))
        bwp = (wsem[0].n, wsem[1].n)
        bp, be = S("sPE").n, S("sEV").n
        for b in range(2 * npr):
            blk = blocks_g[b // 2] if b % 2 == 0 else blocks_u[b // 2]
            if b >= 2:
                w(gp, S("sPE"), bp + b - 1)
            inc(wsem[b % 2], gp.dma_start(
                WB[b % 2][:, 0:nk * 128], blk, max_dma_last_dim=8192), 16)
        for j in range(npr):
            q = 4 * (j % 2)
            if j >= 2:
                w(te, S("sEV"), be + j - 1)
            for half in range(2):
                b = 2 * j + half
                w(te, wsem[b % 2], bwp[b % 2] + 16 * (b // 2 + 1))
                pq = q + 2 * half
                for k in range(nk):
                    lhs = WB[b % 2][:, k * 128:(k + 1) * 128]
                    te.matmul(PS[pq], lhs, rhs_fn(k, 0), start=(k == 0),
                              stop=(k == nk - 1))
                    mm = te.matmul(PS[pq + 1], lhs, rhs_fn(k, 1), start=(k == 0),
                                   stop=(k == nk - 1))
                inc(S("sPE"), mm)
        for j in range(npr):
            consume_pair(j, bp, be)

    bsg8 = S("sSG").n

    def sh_consume(j, bp, be):
        q = 4 * (j % 2)
        w(sc, S("sPE"), bp + 2 * j + 1)
        if j >= 2:
            w(sc, S("sEV"), be + j - 1)
        sc.activation(SG[j % 2][:, 0:512], PS[q], AF.Silu)
        i = sc.activation(SG[j % 2][:, 512:1024], PS[q + 1], AF.Silu)
        inc(S("sSG"), i)
        w(ve, S("sPE"), bp + 2 * j + 2)
        w(ve, S("sSG"), bsg8 + j + 1)
        ve.tensor_tensor(midchunk(j, 0), SG[j % 2][:, 0:512], PS[q + 2], ALU.mult)
        i = ve.tensor_tensor(midchunk(j, 1), SG[j % 2][:, 512:1024], PS[q + 3],
                             ALU.mult)
        inc(S("sEV"), i)

    run_up([sw1_d[j] for j in range(NKS)], [sw1_d[NKS + j] for j in range(NKS)],
           NCH, achunk, sh_consume)
    join(ALL_ENG, ["sEV"])
    emit_tap("mid", MID[:, :], ["sEV"])

    def sh_dn_consume(m, bp, be):
        w(ve, S("sPE"), bp + m + 1)
        ve.tensor_copy(moechunk(m)[:, 0:512], PS[2 * (m % 2)])
        i = ve.tensor_copy(moechunk(m)[:, 512:1024], PS[2 * (m % 2) + 1])
        inc(S("sEV"), i)

    run_proj([sw2_d[m] for m in range(NCH)], NKS, midchunk, sh_dn_consume)
    join(ALL_ENG, ["sEV"])
    stP8.close()
    if stop == "P8":
        return finish()

    ph("P9_experts")
    stP9 = mkstack()
    AE = [sbuf(f"AEs{i}", [128, NKE * T], BF16, stP9) for i in range(2)]
    SG9 = [sbuf(f"SG9s{i}", [128, 1024], BF16, stP9) for i in range(2)]
    CME = [sbuf(f"CMEs{i}", [128, 1024], BF16, stP9) for i in range(2)]

    join((te,), ["sCAST"])
    for ep in range(E // 2):
        e0, e1 = 2 * ep, 2 * ep + 1
        w(te, S("sCMB"))
        if ep >= 1:
            w(te, S("sEV"))
        for i_e, e in enumerate((e0, e1)):
            if i_e == 1:
                w(te, S("sCMB"))
            te.matmul(PS[6][:, :], sel_sb[:, e * 128:(e + 1) * 128],
                      CMF[:, 0:512], start=True, stop=True)
            mm = te.matmul(PS[7][:, :], sel_sb[:, e * 128:(e + 1) * 128],
                           CMF[:, 512:1024], start=True, stop=True)
            inc(S("sPE"), mm)
            w(sc, S("sPE"))
            sc.copy(CME[i_e][:, 0:512], PS[6][:, :])
            i0 = sc.copy(CME[i_e][:, 512:1024], PS[7][:, :])
            inc(S("sCMB"), i0)
        join((te, ve), ["sCMB"])

        for i_e, e in enumerate((e0, e1)):
            bsg = S("sSG").n

            def up_consume(j, bp, be, i_e=i_e, bsg=bsg):
                q = 4 * (j % 2)
                w(sc, S("sPE"), bp + 2 * j + 1)
                if j >= 2:
                    w(sc, S("sEV"), be + j - 1)
                sc.activation(SG9[j % 2][:, 0:512], PS[q], AF.Silu)
                i = sc.activation(SG9[j % 2][:, 512:1024], PS[q + 1], AF.Silu)
                inc(S("sSG"), i)
                w(ve, S("sPE"), bp + 2 * j + 2)
                w(ve, S("sSG"), bsg + j + 1)
                w(ve, S("sEV"), be + j)
                ve.tensor_tensor(CM0[:, 0:512], SG9[j % 2][:, 0:512], PS[q + 2],
                                 ALU.mult)
                ve.tensor_tensor(CM0[:, 512:1024], SG9[j % 2][:, 512:1024],
                                 PS[q + 3], ALU.mult)
                i = ve.tensor_mul(AE[i_e][:, j * T:(j + 1) * T], CM0[:, :],
                                  CME[i_e][:, :])
                inc(S("sEV"), i)

            run_up([w1_d[e, j] for j in range(NKE)],
                   [w1_d[e, NKE + j] for j in range(NKE)],
                   NCH, achunk, up_consume)

        w(te, S("sEV"))
        w(sy, S("sDN"))
        wdsem = (S("sWD"), S("sWD2"), S("sWD3"))
        bwdp = (wdsem[0].n, wdsem[1].n, wdsem[2].n)
        bdn = S("sDN").n
        bev2 = S("sEV2").n
        blocks = [(e_i, m) for m in range(NCH) for e_i in range(2)]
        for b, (e_i, m) in enumerate(blocks):
            if b >= 3:
                w(sy, S("sDN"), bdn + b - 2)
            e = (e0, e1)[e_i]
            inc(wdsem[b % 3], sy.dma_start(
                WD[b % 3][:, :], w2_d[e, m], max_dma_last_dim=8192), 16)
        for m in range(NCH):
            q = 2 * (m % 2)
            if m >= 2:
                w(te, S("sEV2"), bev2 + m - 1)
            for e_i in range(2):
                b = 2 * m + e_i
                w(te, wdsem[b % 3], bwdp[b % 3] + 16 * (b // 3 + 1))
                for k in range(NKE):
                    lhs = WD[b % 3][:, k * 128:(k + 1) * 128]
                    te.matmul(PS[q], lhs,
                              AE[e_i][:, k * T: k * T + 512],
                              start=(e_i == 0 and k == 0),
                              stop=(e_i == 1 and k == NKE - 1))
                    mm = te.matmul(PS[q + 1], lhs,
                                   AE[e_i][:, k * T + 512: (k + 1) * T],
                                   start=(e_i == 0 and k == 0),
                                   stop=(e_i == 1 and k == NKE - 1))
                inc(S("sDN"), mm)
        for m in range(NCH):
            w(ve, S("sDN"), bdn + 2 * (m + 1))
            q = 2 * (m % 2)
            ve.tensor_tensor(moechunk(m)[:, 0:512], moechunk(m)[:, 0:512],
                             PS[q], ALU.add)
            i = ve.tensor_tensor(moechunk(m)[:, 512:1024],
                                 moechunk(m)[:, 512:1024], PS[q + 1], ALU.add)
            inc(S("sEV2"), i)
        join((te,), ["sEV2"])
    join(ALL_ENG, ["sEV2"])
    stP9.close()
    emit_tap("moe", MOE[:, :], ["sEV2"])
    if stop == "P9":
        return finish()

    ph("P10_postLN")
    stP10 = mkstack()
    FF0 = sbuf("FF0", [128, 1024], F32, stP10)
    FF1 = sbuf("FF1", [128, 1024], F32, stP10)
    TP = sbuf("TP", [128, DM], F32, stP10)

    bst10 = S("sST").n
    bc10 = S("sCV").n
    for c in range(NCH):
        if c >= 2:
            w(ve, S("sST"), bst10 + (c - 1))
        inc(S("sCV"), ve.tensor_copy(CM0 if c % 2 == 0 else CM1, moechunk(c)))

    def mchunk_full(c):
        return CM0 if c % 2 == 0 else CM1

    emit_ln_fields(lambda c: mchunk_full(c)[:, :],
                   lambda c, nh: mchunk_full(c)[:, nh * 512:(nh + 1) * 512],
                   "sCV", bc10, double=False, f32_fields=(FF0, FF1))
    join((ve, te), ["sFLD"])
    if stop == "P10":
        return finish()

    ph("P11_out")
    stP11 = mkstack()
    TMo = [sbuf(f"TMo{i}", [128, DM], F32, stP11) for i in range(2)]
    XSL = [sbuf(f"XSL{i}", [128, NCH * 128], F32, stP11) for i in range(2)]
    moe3 = MOE.rearrange("p (c t) -> p c t", c=NCH)
    xsp3 = x_spill.rearrange("p (c t) -> p c t", c=NCH)
    bpe11 = S("sPE").n
    bev11 = S("sEV").n
    qsem3 = (S("sXL"), S("sXL2"))
    bq11p = (qsem3[0].n, qsem3[1].n)
    osem = (S("sOUT"), S("sOUT2"))
    boutp = (osem[0].n, osem[1].n)
    badd = S("sEV2").n
    w(gp, S("sXSP"))
    for j in range(NSLAB):
        if j >= 2:
            w(gp, S("sPE"), bpe11 + j - 1)
        inc(qsem3[j % 2], gp.dma_start(
            XSL[j % 2].rearrange("p (c t) -> p c t", c=NCH),
            xsp3[:, :, j * 128:(j + 1) * 128]), 16)
    for j in range(NSLAB):
        jsl = slice(j * 128, (j + 1) * 128)
        TPv = TP.rearrange("p (c t) -> p c t", c=NCH)
        w(ve, qsem3[j % 2], bq11p[j % 2] + 16 * (j // 2 + 1))
        if j >= 2:
            w(ve, S("sPE"), bpe11 + j - 1)
        ve.tensor_tensor(TPv, moe3[:, :, jsl],
                         FF0[:, jsl].unsqueeze(1).to_broadcast([128, NCH, 128]),
                         ALU.mult)
        ve.tensor_tensor(TPv, TPv,
                         FF1[:, jsl].unsqueeze(1).to_broadcast([128, NCH, 128]),
                         ALU.subtract)
        i0 = ve.tensor_tensor(XSL[j % 2].rearrange("p (c t) -> p c t", c=NCH),
                              XSL[j % 2].rearrange("p (c t) -> p c t", c=NCH),
                              TPv, ALU.add)
        inc(S("sEV2"), i0)
        w(te, S("sEV2"), badd + j + 1)
        if j >= 1:
            w(te, S("sEV"), bev11 + j)
        for c in range(NCH):
            mm = te.transpose(PS[4 + c % 4][:, (c // 4) * 128:(c // 4) * 128 + 128],
                              XSL[j % 2][:, c * 128:(c + 1) * 128], ident)
        inc(S("sPE"), mm)
        w(sc, S("sPE"), bpe11 + j + 1)
        if j >= 2:
            w(sc, osem[j % 2], boutp[j % 2] + 16 * (j // 2))
        for c in range(NCH):
            i0 = sc.copy(TMo[j % 2][:, c * 128:(c + 1) * 128],
                         PS[4 + c % 4][:, (c // 4) * 128:(c // 4) * 128 + 128])
        inc(S("sEV"), i0)
        oq = sy if j % 2 == 0 else sc
        w(oq, S("sEV"), bev11 + j + 1)
        inc(osem[j % 2],
            oq.dma_start(out_d[j * 128:(j + 1) * 128, :], TMo[j % 2]), 16)
    join(ALL_ENG, ["sOUT", "sOUT2"])
    stP11.close()
    stP10.close()
    stCMF.close()
    return finish()



def _mblocks(wT):
    K, M = wT.shape
    nk, nm = K // 128, M // 128
    b = wT.reshape(nk, 128, nm, 128).transpose(2, 1, 0, 3)
    return np.ascontiguousarray(b.reshape(nm, 128, K))


def _prep_inputs(inputs):
    f32 = np.float32
    hs = np.asarray(inputs["L0_attn_hidden_states"], f32).reshape(8192, DM)
    wq = np.asarray(inputs["L0_attn_w_q"], f32)
    wk = np.asarray(inputs["L0_attn_w_k"], f32)
    wv = np.asarray(inputs["L0_attn_w_v"], f32)
    wo = np.asarray(inputs["L0_attn_w_o"], f32)
    gamma = np.asarray(inputs["L0_attn_ln_gamma"], f32)
    beta = np.asarray(inputs["L0_attn_ln_beta"], f32)
    gate_w = np.asarray(inputs["L0_ffn_gate_w"], f32)
    w1 = np.asarray(inputs["L0_ffn_w1"], f32)
    w2 = np.asarray(inputs["L0_ffn_w2"], f32)
    sw1 = np.asarray(inputs["L0_ffn_shared_w1"], f32)
    sw2 = np.asarray(inputs["L0_ffn_shared_w2"], f32)

    com = {}
    com["wq_b"] = _mblocks(wq).astype(BF)
    com["wk_b"] = _mblocks(wk).astype(BF)
    com["wv_b"] = _mblocks(wv).astype(BF)
    com["wo_b"] = _mblocks(wo).astype(BF)
    com["gamma_pp"] = np.ascontiguousarray(gamma.reshape(NCH, 128).T)
    com["beta_pp"] = np.ascontiguousarray(beta.reshape(NCH, 128).T)
    com["gate_wT"] = np.ascontiguousarray(gate_w.T).astype(BF)
    com["sw1_b"] = _mblocks(np.ascontiguousarray(sw1.T)).astype(BF)
    com["sw2_b"] = _mblocks(np.ascontiguousarray(sw2.T)).astype(BF)
    com["w1_b"] = np.stack([_mblocks(np.ascontiguousarray(w1[e].T))
                            for e in range(E)]).astype(BF)
    com["w2_b"] = np.stack([_mblocks(np.ascontiguousarray(w2[e].T))
                            for e in range(E)]).astype(BF)
    com["ident"] = np.eye(128, dtype=f32)
    com["ones_in"] = np.ones((128, 128), dtype=f32)
    sel = np.zeros((E, E * 128), dtype=f32)
    for e in range(E):
        sel[e, e * 128:(e + 1) * 128] = 1.0
    com["sel"] = sel.astype(BF)

    inv_freq = (1.0 / (10000.0 ** (np.arange(0, D, 2, dtype=f32) / D))).astype(f32)
    ang = np.arange(8192, dtype=f32)[:, None] * inv_freq[None, :]
    cos_a = np.cos(ang).astype(f32)
    sin_a = np.sin(ang).astype(f32)
    sgn = np.where((np.arange(128) % 64) < 32, -1.0, 1.0).astype(f32)

    in_maps = []
    for i in range(NCORES):
        s0 = i * T
        m = dict(com)
        m["x_in"] = np.ascontiguousarray(hs[s0:s0 + T, :])
        NH = H * T // 2
        cc = cos_a[s0:s0 + T, :].T.reshape(H, T // 128, 2, 64)
        ss = sin_a[s0:s0 + T, :].T.reshape(H, T // 128, 2, 64)
        cosf = np.concatenate(
            [np.broadcast_to(cc[:, :, 0, :].reshape(1, NH), (64, NH)),
             np.broadcast_to(cc[:, :, 1, :].reshape(1, NH), (64, NH))], axis=0)
        sinf = np.concatenate(
            [np.broadcast_to(ss[:, :, 0, :].reshape(1, NH), (64, NH)),
             np.broadcast_to(ss[:, :, 1, :].reshape(1, NH), (64, NH))], axis=0)
        m["cosf"] = np.ascontiguousarray(cosf).astype(BF)
        m["sinf"] = (sgn[:, None] * sinf).astype(BF)
        in_maps.append(m)
    return in_maps


def kernel(**inputs):
    in_maps = _prep_inputs(inputs)
    nc = build_program()
    res = run_bass_kernel_spmd(nc, in_maps, core_ids=list(range(NCORES)))
    out = np.concatenate([r["out"] for r in res.results], axis=0)
    return out.reshape(1, 8192, DM).astype(np.float32)


# version 15
# speedup vs baseline: 1.1928x; 1.1876x over previous
import numpy as np
import ml_dtypes
from contextlib import ExitStack

import concourse.bass as bass
import concourse.mybir as mybir
from concourse.bass_utils import run_bass_kernel_spmd

F32 = mybir.dt.float32
BF16 = mybir.dt.bfloat16
AF = mybir.ActivationFunctionType
ALU = mybir.AluOpType
AX = mybir.AxisListType
BF = ml_dtypes.bfloat16

NCORES = 8
T = 1024
DM = 2048
NCH = DM // 128
H, D = 32, 64
E = 16
I_E = 1408
I_S = 2816
NKE = I_E // 128
NKS = I_S // 128
EPS = 1e-5
SM_SCALE = 0.125

TS = 128
NSLAB = T // TS
GTOK = 16
NGRP = TS // GTOK


class SemC:
    __slots__ = ("h", "n")

    def __init__(self, h):
        self.h = h
        self.n = 0


def build_program(taps=(), stop=None, p3_slabs=NSLAB, p3_mode='full'):
    nc = bass.Bass("TRN2", target_bir_lowering=False)
    taps = set(taps)

    def din(name, shape, dt=F32):
        return nc.dram_tensor(name, list(shape), dt, kind="ExternalInput").ap()

    x_in = din("x_in", (T, DM))
    wq_d = din("wq_b", (NCH, 128, DM), BF16)
    wk_d = din("wk_b", (NCH, 128, DM), BF16)
    wv_d = din("wv_b", (NCH, 128, DM), BF16)
    wo_d = din("wo_b", (NCH, 128, DM), BF16)
    gam_d = din("gamma_pp", (128, NCH))
    bet_d = din("beta_pp", (128, NCH))
    gate_d = din("gate_wT", (DM, E), BF16)
    sw1_d = din("sw1_b", (2 * NKS, 128, DM), BF16)
    sw2_d = din("sw2_b", (NCH, 128, I_S), BF16)
    w1_d = din("w1_b", (E, 2 * NKE, 128, DM), BF16)
    w2_d = din("w2_b", (E, NCH, 128, I_E), BF16)
    id_d = din("ident", (128, 128))
    ones_d = din("ones_in", (128, 128))
    sel_d = din("sel", (E, E * 128), BF16)
    cosf_d = din("cosf", (128, H * T // 2), BF16)
    sinf_d = din("sinf", (128, H * T // 2), BF16)

    out_d = nc.dram_tensor("out", [T, DM], F32, kind="ExternalOutput").ap()

    q_scr = nc.dram_tensor("q_scr", [DM, T], BF16, kind="Internal").ap()
    k_scr = nc.dram_tensor("k_scr", [DM, T], BF16, kind="Internal").ap()
    v_scr = nc.dram_tensor("v_scr", [DM, T], BF16, kind="Internal").ap()
    o_scr = nc.dram_tensor("o_scr", [DM, T], BF16, kind="Internal").ap()
    x_spill = nc.dram_tensor("x_spill", [128, NCH * T], F32, kind="Internal").ap()

    tap_t = {}
    tap_shapes = {
        "a": ((128, NCH * T), BF16), "f": ((128, NCH * T), BF16),
        "ao": ((128, NCH * T), BF16), "x1": ((128, NCH * T), F32),
        "mid": ((128, NKS * T), BF16), "moe": ((128, NCH * T), F32),
        "cmb": ((E, T), BF16), "qs": ((DM, T), BF16), "ks": ((DM, T), BF16),
        "vs": ((DM, T), BF16), "os": ((DM, T), BF16), "logits": ((128, 128), F32),
        "rot": ((128, 4096), BF16), "osl": ((64, H * TS), BF16),
        "eb": ((64, 512), BF16), "ds": ((64, 512), F32),
    }
    for name in taps:
        shp, dt = tap_shapes[name]
        tap_t[name] = nc.dram_tensor("tap_" + name, list(shp), dt,
                                     kind="ExternalOutput").ap()

    st = ExitStack()
    all_stacks = []

    def mkstack():
        s_ = ExitStack()
        all_stacks.append(s_)
        return s_

    def sbuf(name, shape, dt, stack=None):
        return (stack or st).enter_context(nc.sbuf_tensor(name, shape, dt)).ap()

    PS = [st.enter_context(nc.psum_tensor(f"psb{i}", [128, 512], F32)).ap()
          for i in range(8)]

    sems = {}
    for s in ("sW", "sPE", "sEV", "sEV2", "sSQ", "sST", "sFLD", "sCAST",
              "sSL", "sSLg", "sROPE", "sGR", "sSUB", "sEXP", "sSM", "sPV",
              "sOEV", "sODMA", "sQD", "sXSP", "sOUT", "sAUX", "sCMB",
              "sTAP", "sSG", "sDN", "sWD", "sVC", "sW2", "sQD2", "sWD2",
              "sOUT2", "sCV", "sXL", "sXL2", "sWD3", "sSL2", "sRK"):
        sems[s] = SemC(st.enter_context(nc.semaphore(s)))

    te, ve, sc, gp, sy = nc.tensor, nc.vector, nc.scalar, nc.gpsimd, nc.sync
    ALL_ENG = (te, ve, sc, gp, sy)

    def S(n):
        return sems[n]

    def inc(sm, inst, k=1):
        inst.then_inc(sm.h, k)
        sm.n += k
        return inst

    def w(eng, sm, tgt=None):
        eng.wait_ge(sm.h, sm.n if tgt is None else tgt)

    def join(engines, sem_names):
        for e_ in engines:
            for sn in sem_names:
                w(e_, S(sn))

    def esync(eng, inst):
        inc(S("sVC"), inst)
        w(eng, S("sVC"))
        return inst

    def vsync(inst):
        return esync(ve, inst)

    _scope = [None]

    def ph(name):
        if _scope[0] is not None:
            nc.leave_named_scope(_scope[0][0], _scope[0][1], False)
        _scope[0] = (name, nc.enter_named_scope(name, False)[0])

    ident = sbuf("identsb", [128, 128], F32)
    identb = sbuf("identb", [128, 128], BF16)
    ones_bf = sbuf("ones_bf", [128, 128], BF16)
    gam = sbuf("gam", [128, NCH], F32)
    bet = sbuf("bet", [128, NCH], F32)
    gate_sb = sbuf("gate_sb", [128, E * NCH], BF16)
    TNa = sbuf("TNa", [65, 1024], F32)
    TNb = sbuf("TNb", [65, 1024], F32)
    TNc = sbuf("TNc", [65, 1024], F32)
    TNbf = sbuf("TNbf", [1, 2048], BF16)
    FB0 = sbuf("FB0", [128, 1024], BF16)
    FB1 = sbuf("FB1", [128, 1024], BF16)
    CM0 = sbuf("CM0", [128, 1024], BF16)
    CM1 = sbuf("CM1", [128, 1024], BF16)
    SQ = [sbuf(f"SQst{i}", [128, 1024], BF16) for i in range(2)]
    ACT = sbuf("ACT", [128, NCH * T], BF16)
    WB = [sbuf(f"WB{i}", [128, NKS * 128], BF16) for i in range(2)]
    WD = [sbuf(f"WD{i}", [128, NKE * 128], BF16) for i in range(3)]

    def afull(c):
        return ACT[:, c * T:(c + 1) * T]

    def achunk(c, nh):
        return ACT[:, c * T + nh * 512: c * T + (nh + 1) * 512]

    def emit_ln_fields(chunk_full, chunk_half, ready, ready_base, double,
                       f32_fields=None, ready_step=1):
        bsq, bst = S("sSQ").n, S("sST").n
        for c in range(NCH):
            w(sc, S(ready), ready_base + (c + 1) * ready_step)
            if c >= 2:
                w(sc, S("sST"), bst + (c - 1))
            inc(S("sSQ"), sc.activation(SQ[c % 2][:, :], chunk_full(c), AF.Square))
        for c in range(NCH):
            w(te, S(ready), ready_base + (c + 1) * ready_step)
            w(te, S("sSQ"), bsq + c + 1)
            for nh in range(2):
                te.matmul(PS[4 + nh][0:1, :], ones_bf[:, 0:1], chunk_half(c, nh),
                          start=(c == 0), stop=(c == NCH - 1))
            for nh in range(2):
                mm = te.matmul(PS[4 + nh][32:33, :], ones_bf[:, 0:1],
                               SQ[c % 2][:, nh * 512:(nh + 1) * 512],
                               start=(c == 0), stop=(c == NCH - 1))
            inc(S("sST"), mm)
        w(ve, S("sST"))
        a0 = TNa[0:1, :]
        a32 = TNa[32:33, :]
        b0 = TNb[0:1, :]
        b32 = TNb[32:33, :]
        c0 = TNc[0:1, :]
        c32 = TNc[32:33, :]
        ve.tensor_copy(TNa[0:1, 0:512], PS[4][0:1, :])
        ve.tensor_copy(TNa[0:1, 512:1024], PS[5][0:1, :])
        ve.tensor_copy(TNa[32:33, 0:512], PS[4][32:33, :])
        ve.tensor_copy(TNa[32:33, 512:1024], PS[5][32:33, :])
        mu, cc = b0, c0
        ve.tensor_scalar(mu, a0, 1.0 / DM, None, ALU.mult)
        ve.tensor_scalar(c0, a32, 1.0 / DM, None, ALU.mult)
        ve.tensor_mul(a0, mu, mu)
        ve.tensor_sub(c0, c0, a0)
        i1 = ve.tensor_scalar(a0, c0, EPS, None, ALU.add)
        inc(S("sEV"), i1)
        w(sc, S("sEV"))
        if double:
            esync(sc, sc.activation(b32, a0, AF.Ln))
            sc.activation(a32, b32, AF.Exp, scale=-0.5)
            i2 = sc.activation(a0, b32, AF.Exp, scale=-1.0)
            inc(S("sEXP"), i2)
            w(ve, S("sEXP"))
            ve.tensor_mul(a0, c0, a0)
            i3 = ve.tensor_scalar(a0, a0, EPS, None, ALU.add)
            inc(S("sEV"), i3)
            w(sc, S("sEV"))
            esync(sc, sc.activation(b32, a0, AF.Ln))
            i4 = sc.activation(c32, b32, AF.Exp, scale=-0.5)
            inc(S("sEXP"), i4)
            w(ve, S("sEXP"))
            ve.tensor_mul(cc, a32, c32)
        else:
            esync(sc, sc.activation(b32, a0, AF.Ln))
            i2 = sc.activation(cc, b32, AF.Exp, scale=-0.5)
            inc(S("sEXP"), i2)
            w(ve, S("sEXP"))
        ve.tensor_mul(a0, mu, cc)
        ve.tensor_copy(TNbf[0:1, 0:1024], cc)
        i5 = ve.tensor_copy(TNbf[0:1, 1024:2048], a0)
        inc(S("sEV"), i5)
        w(te, S("sEV"))
        te.matmul(PS[6][:, :], ones_bf[0:1, :], TNbf[0:1, 0:512],
                  start=True, stop=True)
        te.matmul(PS[7][:, :], ones_bf[0:1, :], TNbf[0:1, 512:1024],
                  start=True, stop=True)
        te.matmul(PS[4][:, :], ones_bf[0:1, :], TNbf[0:1, 1024:1536],
                  start=True, stop=True)
        i6 = te.matmul(PS[5][:, :], ones_bf[0:1, :], TNbf[0:1, 1536:2048],
                       start=True, stop=True)
        inc(S("sST"), i6)
        w(ve, S("sST"))
        ve.tensor_copy(FB0[:, 0:512], PS[6][:, :])
        ve.tensor_copy(FB0[:, 512:1024], PS[7][:, :])
        ve.tensor_copy(FB1[:, 0:512], PS[4][:, :])
        i7 = ve.tensor_copy(FB1[:, 512:1024], PS[5][:, :])
        if f32_fields is not None:
            FF0, FF1 = f32_fields
            ve.tensor_copy(FF0[:, 0:512], PS[6][:, :])
            ve.tensor_copy(FF0[:, 512:1024], PS[7][:, :])
            ve.tensor_copy(FF1[:, 0:512], PS[4][:, :])
            i7 = ve.tensor_copy(FF1[:, 512:1024], PS[5][:, :])
        inc(S("sFLD"), i7)

    def run_proj(blocks_dram, nk, rhs_fn, consume, wbufs=None, dma_eng=None):
        wbufs = wbufs or WB
        dma_eng = dma_eng or gp
        nblk = len(blocks_dram)
        w(te, S("sEV"))
        w(dma_eng, S("sPE"))
        wsem = (S("sW"), S("sW2"))
        bwp = (wsem[0].n, wsem[1].n)
        bp, be = S("sPE").n, S("sEV").n
        for m in range(nblk):
            if m >= 2:
                w(dma_eng, S("sPE"), bp + m - 1)
            inc(wsem[m % 2], dma_eng.dma_start(
                wbufs[m % 2][:, 0:nk * 128], blocks_dram[m],
                max_dma_last_dim=8192), 16)
        for m in range(nblk):
            w(te, wsem[m % 2], bwp[m % 2] + 16 * (m // 2 + 1))
            if m >= 2:
                w(te, S("sEV"), be + m - 1)
            pA, pB = PS[2 * (m % 2)], PS[2 * (m % 2) + 1]
            for k in range(nk):
                lhs = wbufs[m % 2][:, k * 128:(k + 1) * 128]
                te.matmul(pA, lhs, rhs_fn(k, 0), start=(k == 0), stop=(k == nk - 1))
                mm = te.matmul(pB, lhs, rhs_fn(k, 1), start=(k == 0),
                               stop=(k == nk - 1))
            inc(S("sPE"), mm)
        for m in range(nblk):
            consume(m, bp, be)

    def emit_tap(name, src_ap, wait_sems):
        if name not in tap_t:
            return
        for sn in wait_sems:
            w(sy, S(sn))
        inc(S("sTAP"), sy.dma_start(tap_t[name][:, :], src_ap), 16)

    def finish():
        if _scope[0] is not None:
            nc.leave_named_scope(_scope[0][0], _scope[0][1], False)
            _scope[0] = None
        join((sy,), ["sOUT", "sOUT2", "sTAP"])
        join((te, ve, sc, gp), ["sOUT", "sOUT2"])
        for stk in reversed(all_stacks):
            stk.close()
        st.close()
        return nc

    BIGA = sbuf("BIGA", [128, NCH * T], F32)
    X = BIGA

    def xchunk(c):
        return X[:, c * T:(c + 1) * T]

    ph("P0_in")
    stP0 = mkstack()
    TM = [sbuf(f"TM{i}", [128, DM], F32, stP0) for i in range(2)]

    inc(S("sAUX"), sy.dma_start(ident, id_d), 16)
    inc(S("sAUX"), sy.dma_start(gam, gam_d), 16)
    inc(S("sAUX"), sy.dma_start(bet, bet_d), 16)
    inc(S("sCAST"), gp.dma_start(identb, id_d), 16)
    inc(S("sCAST"), gp.dma_start(ones_bf, ones_d), 16)
    inc(S("sCAST"), gp.dma_start(
        gate_sb.rearrange("p (k e) -> p k e", k=NCH),
        gate_d.rearrange("(k p) e -> p k e", p=128)), 16)

    qsem = (S("sQD"), S("sQD2"))
    bq0p = (qsem[0].n, qsem[1].n)
    bp0, be0 = S("sPE").n, S("sEV").n
    for j in range(NSLAB):
        if j >= 2:
            w(sy, S("sPE"), bp0 + j - 1)
        inc(qsem[j % 2], sy.dma_start(TM[j % 2], x_in[j * 128:(j + 1) * 128, :]), 16)
    for j in range(NSLAB):
        w(te, qsem[j % 2], bq0p[j % 2] + 16 * (j // 2 + 1))
        if j == 0:
            w(te, S("sAUX"))
        if j >= 1:
            w(te, S("sEV"), be0 + j)
        for c in range(NCH):
            mm = te.transpose(PS[4 + c % 4][:, (c // 4) * 128:(c // 4) * 128 + 128],
                              TM[j % 2][:, c * 128:(c + 1) * 128], ident)
        inc(S("sPE"), mm)
        w(ve, S("sPE"), bp0 + j + 1)
        for c in range(NCH):
            i0 = ve.tensor_copy(xchunk(c)[:, j * 128:(j + 1) * 128],
                                PS[4 + c % 4][:, (c // 4) * 128:(c // 4) * 128 + 128])
        inc(S("sEV"), i0)
    join(ALL_ENG, ["sEV", "sQD", "sQD2", "sAUX", "sCAST"])
    stP0.close()
    if stop == "P0":
        return finish()

    ph("P1_preLN")
    bc1 = S("sCV").n
    for c in range(NCH):
        inc(S("sCV"), ve.tensor_copy(afull(c), xchunk(c)))
    emit_ln_fields(afull, achunk, "sCV", bc1, double=True)
    join((ve,), ["sFLD"])
    for c in range(NCH):
        ve.tensor_mul(CM0[:, :], afull(c), FB0)
        ve.tensor_sub(CM0[:, :], CM0[:, :], FB1)
        i0 = ve.tensor_scalar(afull(c), CM0[:, :], gam[:, c:c + 1],
                              bet[:, c:c + 1], ALU.mult, ALU.add)
    inc(S("sEV"), i0)
    join(ALL_ENG, ["sEV"])
    emit_tap("a", ACT[:, :], ["sEV"])
    if stop == "P1":
        return finish()

    ph("P2_qkv")
    stP2 = mkstack()
    QS = [sbuf(f"QSst{i}", [128, 1024], BF16, stP2) for i in range(2)]

    qk_mark = None
    for pj, (wmat, scr) in enumerate(((wq_d, q_scr), (wk_d, k_scr),
                                      (wv_d, v_scr))):
        qsem2 = (S("sQD"), S("sQD2"))
        bqdp = (qsem2[0].n, qsem2[1].n)

        def consume(m, bp, be, scr=scr, qsem2=qsem2, bqdp=bqdp):
            w(ve, S("sPE"), bp + m + 1)
            w(ve, qsem2[m % 2], bqdp[m % 2] + 16 * (m // 2))
            ve.tensor_copy(QS[m % 2][:, 0:512], PS[2 * (m % 2)])
            i = ve.tensor_copy(QS[m % 2][:, 512:1024], PS[2 * (m % 2) + 1])
            inc(S("sEV"), i)
            w(sy, S("sEV"))
            inc(qsem2[m % 2],
                sy.dma_start(scr[m * 128:(m + 1) * 128, :], QS[m % 2]), 16)

        run_proj([wmat[m] for m in range(NCH)], NCH, achunk, consume)
        if pj == 1:
            qk_mark = (S("sQD").n, S("sQD2").n)
    pe_full = S("sPE").n
    emit_tap("qs", q_scr[:, :], ["sQD", "sQD2"])
    emit_tap("ks", k_scr[:, :], ["sQD", "sQD2"])
    emit_tap("vs", v_scr[:, :], ["sQD", "sQD2"])
    if stop == "P2":
        return finish()

    ph("P3_attn")
    NGR3 = 4
    QKR = ACT[:, 0:2048]
    QKP = ACT[:, 2048:4096]
    KKP = ACT[:, 4096:6144]
    CSF = ACT[:, 6144:8192]
    SNF = ACT[:, 8192:10240]
    KKR = [ACT[:, 10240 + 2048 * i: 12288 + 2048 * i] for i in range(2)]

    q3 = q_scr.rearrange("(h d) t -> d h t", d=D)
    k3 = k_scr.rearrange("(h d) t -> d h t", d=D)
    qp4 = q_scr.rearrange("(h d2 p2) t -> p2 d2 h t", d2=32, p2=2)
    kp4 = k_scr.rearrange("(h d2 p2) t -> p2 d2 h t", d2=32, p2=2)
    v3 = v_scr.rearrange("(h d) t -> h d t", d=D)
    o3 = o_scr.rearrange("(h d) t -> d h t", d=D)
    cs4 = cosf_d.rearrange("p (h s u) -> p h s u", h=H, s=NSLAB)
    sn4 = sinf_d.rearrange("p (h s u) -> p h s u", h=H, s=NSLAB)

    bsl = S("sSL").n
    bsl2 = S("sSL2").n
    bslg = S("sSLg").n
    brope = S("sROPE").n
    brk = S("sRK").n
    bgr = S("sGR").n
    bexp = S("sEXP").n
    bsm = S("sSM").n
    bpv = S("sPV").n
    boev = S("sOEV").n
    bodma = S("sODMA").n

    def hv(ap):
        return ap.rearrange("p (h u) -> p h u", h=H)

    def emit_loads_syc(s):
        tsl0 = slice(s * TS, s * TS + 64)
        tsl1 = slice(s * TS + 64, (s + 1) * TS)
        if s >= 1:
            w(sy, S("sROPE"), brope + s)
            w(sy, S("sRK"), brk + s)
        if s >= 2:
            w(sc, S("sGR"), bgr + NGR3 * (s - 1))
        inc(S("sSL"), sy.dma_start(hv(QKR[0:64, :]), q3[:, :, tsl0]), 16)
        inc(S("sSL"), sy.dma_start(hv(QKR[64:128, :]), q3[:, :, tsl1]), 16)
        inc(S("sSL"), sy.dma_start(hv(KKP[0:32, :]), kp4[1][:, :, tsl0]), 16)
        inc(S("sSL"), sy.dma_start(hv(KKP[32:64, :]), kp4[0][:, :, tsl0]), 16)
        inc(S("sSL"), sy.dma_start(hv(KKP[64:96, :]), kp4[1][:, :, tsl1]), 16)
        inc(S("sSL"), sy.dma_start(hv(KKP[96:128, :]), kp4[0][:, :, tsl1]), 16)
        inc(S("sSL"), sy.dma_start(hv(QKP[0:32, :]), qp4[1][:, :, tsl0]), 16)
        inc(S("sSL"), sy.dma_start(hv(QKP[32:64, :]), qp4[0][:, :, tsl0]), 16)
        inc(S("sSL"), sy.dma_start(hv(QKP[64:96, :]), qp4[1][:, :, tsl1]), 16)
        inc(S("sSL"), sy.dma_start(hv(QKP[96:128, :]), qp4[0][:, :, tsl1]), 16)
        inc(S("sSL2"), sc.dma_start(hv(KKR[s % 2][0:64, :]), k3[:, :, tsl0]), 16)
        inc(S("sSL2"), sc.dma_start(hv(KKR[s % 2][64:128, :]), k3[:, :, tsl1]), 16)

    w(sy, S("sPE"), pe_full)
    w(sy, S("sQD"), qk_mark[0])
    w(sy, S("sQD2"), qk_mark[1])
    w(sc, S("sPE"), pe_full)
    w(sc, S("sQD"), qk_mark[0])
    w(sc, S("sQD2"), qk_mark[1])
    emit_loads_syc(0)
    join(ALL_ENG, ["sQD", "sQD2", "sEV"])
    stP2.close()
    emit_tap("qs", q_scr[:, :], ["sQD", "sQD2"])
    emit_tap("ks", k_scr[:, :], ["sQD", "sQD2"])
    emit_tap("vs", v_scr[:, :], ["sQD", "sQD2"])

    stP3 = mkstack()
    QPK = [sbuf(f"QPKs{i}", [128, 4096], BF16, stP3) for i in range(2)]
    VS2 = [sbuf(f"VS2s{i}", [64, 4096], BF16, stP3) for i in range(2)]
    OSL = sbuf("OSL", [64, H * TS], BF16, stP3)
    ET2 = [sbuf(f"ET2s{i}", [64, 1024], BF16, stP3) for i in range(2)]
    EB = [sbuf(f"EBs{i}", [64, 512], BF16, stP3) for i in range(2)]
    MXZ = sbuf("MXZ", [64, 64], F32, stP3)

    QKRv = QKR.rearrange("p (h u) -> p u h", h=H)
    QKPv = QKP.rearrange("p (h u) -> p u h", h=H)
    QPK4 = [QPK[i].rearrange("p (u two h) -> p u two h", two=2, h=H)
            for i in range(2)]
    KKRv = [KKR[i].rearrange("p (h u) -> p u h", h=H) for i in range(2)]
    VS2v = [VS2[i].rearrange("p (d u) -> p d u", d=D) for i in range(2)]
    ET2v = [ET2[i].rearrange("p (u two h) -> p u two h", two=2, h=H)
            for i in range(2)]
    OSL3 = OSL.rearrange("p (h t) -> p h t", h=H)
    OSL4 = OSL.rearrange("p (h two u) -> p h two u", h=H, two=2)

    for i in range(2):
        ve.memzero(QPK4[i][64:128, :, 0, :])
        ve.memzero(QPK4[i][0:64, :, 1, :])
        ve.memzero(ET2v[i][32:64, :, 0, :])
        i0 = ve.memzero(ET2v[i][0:32, :, 1, :])
    vsync(i0)

    def emit_loads_gp(s):
        tsl0 = slice(s * TS, s * TS + 64)
        tsl1 = slice(s * TS + 64, (s + 1) * TS)
        if s >= 1:
            w(gp, S("sROPE"), brope + s)
        if s >= 2:
            w(gp, S("sPV"), bpv + NGR3 * (s - 1))
        inc(S("sSLg"), gp.dma_start(
            VS2[s % 2][0:32, :].rearrange("p (d u) -> p d u", d=D),
            v3[:, :, tsl0]), 16)
        inc(S("sSLg"), gp.dma_start(
            VS2[s % 2][32:64, :].rearrange("p (d u) -> p d u", d=D),
            v3[:, :, tsl1]), 16)
        inc(S("sSLg"), gp.dma_start(hv(CSF), cs4[:, :, s, :]), 16)
        inc(S("sSLg"), gp.dma_start(hv(SNF), sn4[:, :, s, :]), 16)

    def emit_rope_q(s):
        w(ve, S("sSL"), bsl + 16 * 10 * (s + 1))
        w(ve, S("sSLg"), bslg + 16 * 4 * (s + 1))
        if s >= 2:
            w(ve, S("sGR"), bgr + NGR3 * (s - 1))
        ve.tensor_mul(QKR[:, :], QKR[:, :], CSF[:, :])
        ve.tensor_mul(QKP[:, :], QKP[:, :], SNF[:, :])
        ve.tensor_add(QPK4[s % 2][0:64, :, 0, :], QKRv[0:64], QKPv[0:64])
        i0 = ve.tensor_add(QPK4[s % 2][64:128, :, 1, :], QKRv[64:128],
                           QKPv[64:128])
        inc(S("sROPE"), i0)

    def emit_rope_k(s):
        w(gp, S("sSL2"), bsl2 + 16 * 2 * (s + 1))
        w(gp, S("sSL"), bsl + 16 * (10 * s + 6))
        w(gp, S("sSLg"), bslg + 16 * 4 * (s + 1))
        if s >= 2:
            w(gp, S("sGR"), bgr + NGR3 * (s - 1))
        gp.tensor_mul(KKR[s % 2][:, :], KKR[s % 2][:, :], CSF[:, :])
        gp.tensor_mul(KKP[:, :], KKP[:, :], SNF[:, :])
        i0 = gp.tensor_add(KKR[s % 2][:, :], KKR[s % 2][:, :], KKP[:, :])
        inc(S("sRK"), i0)

    def emit_gram(s, g):
        gg = NGR3 * s + g
        w(te, S("sROPE"), brope + s + 1)
        w(te, S("sRK"), brk + s + 1)
        if gg >= 2:
            w(te, S("sEXP"), bexp + gg - 1)
        for i in range(16):
            u = 16 * g + i
            mm = te.matmul(PS[gg % 2][0:64, i * 32:(i + 1) * 32],
                           QPK[s % 2][:, u * 64:(u + 1) * 64],
                           KKRv[s % 2][:, u, :], start=True, stop=True)
        inc(S("sGR"), mm)

    def emit_softmax(s, g):
        gg = NGR3 * s + g
        w(sc, S("sGR"), bgr + gg + 1)
        if gg >= 2:
            w(sc, S("sSM"), bsm + gg - 1)
        i2 = sc.activation(EB[gg % 2][:, :], PS[gg % 2][0:64, :], AF.Exp,
                           scale=SM_SCALE)
        inc(S("sEXP"), i2)
        w(ve, S("sEXP"), bexp + gg + 1)
        eb3 = EB[gg % 2].rearrange("p (u h) -> p u h", u=16)
        zr = MXZ[:, 32 + 16 * (gg % 2):48 + 16 * (gg % 2)]
        vsync(ve.reduce_sum(zr.unsqueeze(2), eb3, axis=AX.X))
        vsync(ve.reciprocal(zr, zr))
        ve.tensor_tensor(eb3, eb3, zr.unsqueeze(2).to_broadcast([64, 16, 32]),
                         ALU.mult)
        if gg >= 2:
            w(ve, S("sPV"), bpv + gg - 1)
        ve.transpose(ET2v[gg % 2][0:32, :, 0, :], EB[gg % 2][0:32, :])
        i3 = ve.transpose(ET2v[gg % 2][32:64, :, 1, :], EB[gg % 2][32:64, :])
        inc(S("sSM"), i3)

    def emit_pv(s, g):
        gg = NGR3 * s + g
        w(te, S("sSM"), bsm + gg + 1)
        w(te, S("sSLg"), bslg + 16 * 4 * (s + 1))
        if gg >= 2:
            w(te, S("sOEV"), boev + 2 * (gg - 1))
        for i in range(16):
            u = 16 * g + i
            bank = PS[2 + 2 * (g % 2) + i // 8]
            mm = te.matmul(bank[0:64, (i % 8) * 64:(i % 8) * 64 + 64],
                           VS2v[s % 2][:, :, u],
                           ET2[gg % 2][:, i * 64:(i + 1) * 64],
                           start=True, stop=True)
        inc(S("sPV"), mm)

    def emit_oev(s, g):
        gg = NGR3 * s + g
        w(sc, S("sPV"), bpv + gg + 1)
        if g == 0 and s >= 1:
            w(sc, S("sODMA"), bodma + 16 * s)
        for b in range(2):
            src = PS[2 + 2 * (g % 2) + b][0:64, :].rearrange(
                "p (u two h) -> p h two u", u=8, two=2)
            i0 = sc.copy(OSL4[:, :, :, g * 16 + b * 8: g * 16 + b * 8 + 8], src)
            inc(S("sOEV"), i0)

    emit_loads_gp(0)
    emit_rope_k(0)
    emit_rope_q(0)
    for s in range(NSLAB):
        if s + 1 < NSLAB:
            emit_loads_syc(s + 1)
            emit_loads_gp(s + 1)
            emit_rope_k(s + 1)
        emit_gram(s, 0)
        emit_softmax(s, 0)
        emit_gram(s, 1)
        emit_softmax(s, 1)
        if s + 1 < NSLAB:
            emit_rope_q(s + 1)
        emit_pv(s, 0)
        emit_gram(s, 2)
        emit_softmax(s, 2)
        emit_pv(s, 1)
        emit_oev(s, 0)
        emit_gram(s, 3)
        emit_softmax(s, 3)
        emit_pv(s, 2)
        emit_oev(s, 1)
        emit_pv(s, 3)
        emit_oev(s, 2)
        emit_oev(s, 3)
        w(sy, S("sOEV"), boev + 8 * (s + 1))
        inc(S("sODMA"), sy.dma_start(
            o3[:, :, slice(s * TS, (s + 1) * TS)], OSL3[:, :, :]), 16)

    emit_tap("rot", QPK[1][:, :], ["sROPE"])
    emit_tap("eb", EB[1][:, :], ["sSM"])
    emit_tap("osl", OSL[:, :], ["sOEV"])
    join(ALL_ENG, ["sODMA", "sTAP"])
    stP3.close()
    emit_tap("os", o_scr[:, :], ["sODMA"])
    if stop == "P3":
        return finish()

    ph("P4_wo")
    stP4 = mkstack()
    OFM = sbuf("OFM", [128, NCH * T], BF16, stP4)
    ofm3 = OFM.rearrange("p (c t) -> p c t", c=NCH)
    osc3 = o_scr.rearrange("(c p) t -> p c t", p=128)
    w(sy, S("sODMA"))
    w(gp, S("sODMA"))
    w(sc, S("sODMA"))
    inc(S("sQD"), sy.dma_start(ofm3[:, 0:6], osc3[:, 0:6]), 16)
    inc(S("sQD2"), gp.dma_start(ofm3[:, 6:11], osc3[:, 6:11]), 16)
    inc(S("sQD"), sc.dma_start(ofm3[:, 11:16], osc3[:, 11:16]), 16)
    join((te, ve, sc, gp), ["sQD", "sQD2"])

    def ofm_chunk(k, nh):
        return OFM[:, k * T + nh * 512: k * T + (nh + 1) * 512]

    be4 = S("sEV").n

    def wo_consume(m, bp, be):
        w(ve, S("sPE"), bp + m + 1)
        ve.tensor_copy(achunk(m, 0), PS[2 * (m % 2)])
        i = ve.tensor_copy(achunk(m, 1), PS[2 * (m % 2) + 1])
        inc(S("sEV"), i)

    run_proj([wo_d[m] for m in range(NCH)], NCH, ofm_chunk, wo_consume)
    emit_ln_fields(afull, achunk, "sEV", be4, double=False)
    join(ALL_ENG, ["sFLD", "sEV"])
    stP4.close()
    emit_tap("ao", ACT[:, :], ["sFLD"])
    if stop == "P4":
        return finish()

    ph("P5_resid")
    stP5 = mkstack()
    Y32 = sbuf("Y32", [128, 1024], F32, stP5)
    for c in range(NCH):
        ve.tensor_mul(CM0[:, :], afull(c), FB0)
        ve.tensor_sub(Y32[:, :], CM0[:, :], FB1)
        i0 = ve.tensor_add(xchunk(c), xchunk(c), Y32[:, :])
    inc(S("sEV"), i0)
    join(ALL_ENG, ["sEV"])
    stP5.close()
    emit_tap("x1", X[:, :], ["sEV"])
    if stop == "P5":
        return finish()

    ph("P6_ffnLN")
    bc6 = S("sCV").n
    for c in range(NCH):
        inc(S("sCV"), ve.tensor_copy(afull(c), xchunk(c)))
    emit_ln_fields(afull, achunk, "sCV", bc6, double=False)
    join((ve,), ["sFLD"])
    for c in range(NCH):
        ve.tensor_mul(CM0[:, :], afull(c), FB0)
        i0 = ve.tensor_sub(afull(c), CM0[:, :], FB1)
    inc(S("sEV"), i0)
    join((sy,), ["sEV"])
    inc(S("sXSP"), sy.dma_start(x_spill[:, 0:8192], X[:, 0:8192]), 16)
    inc(S("sXSP"), sy.dma_start(x_spill[:, 8192:16384], X[:, 8192:16384]), 16)
    join(ALL_ENG, ["sXSP", "sEV"])
    emit_tap("f", ACT[:, :], ["sEV"])
    if stop == "P6":
        return finish()

    ph("P7_gate")
    MOE = BIGA

    def moechunk(c):
        return MOE[:, c * T:(c + 1) * T]

    stCMF = mkstack()
    CMF = sbuf("CMF", [16, 1024], BF16, stCMF)
    sel_sb = sbuf("sel_sb", [E, E * 128], BF16, stCMF)
    inc(S("sCAST"), gp.dma_start(sel_sb, sel_d), 16)
    stP7 = mkstack()
    GL = sbuf("GL", [16, 1024], F32, stP7)
    GLT = sbuf("GLT", [128, 128], F32, stP7)
    GLE = sbuf("GLE", [128, 128], F32, stP7)
    MX8 = sbuf("MX8", [128, 16], F32, stP7)
    MSK = sbuf("MSK", [128, 16], F32, stP7)
    CMBb = sbuf("CMBb", [128, 128], BF16, stP7)

    for k in range(NCH):
        te.matmul(PS[4][0:16, :], gate_sb[:, k * E:(k + 1) * E], achunk(k, 0),
                  start=(k == 0), stop=(k == NCH - 1))
        mm = te.matmul(PS[5][0:16, :], gate_sb[:, k * E:(k + 1) * E], achunk(k, 1),
                       start=(k == 0), stop=(k == NCH - 1))
    inc(S("sPE"), mm)
    w(ve, S("sPE"))
    ve.tensor_copy(GL[:, 0:512], PS[4][0:16, :])
    i0 = ve.tensor_copy(GL[:, 512:1024], PS[5][0:16, :])
    inc(S("sEV"), i0)
    w(te, S("sEV"))
    for j in range(NSLAB):
        mm = te.transpose(PS[6][:, j * 16:(j + 1) * 16],
                          GL[:, j * 128:(j + 1) * 128], ident[0:16, 0:16])
    inc(S("sPE"), mm)
    w(ve, S("sPE"))
    vsync(ve.tensor_copy(GLT[:, :], PS[6][:, 0:128]))
    g3 = GLT.rearrange("p (s e) -> p s e", s=NSLAB)
    mx8s = MX8[:, 0:8]
    vsync(ve.reduce_max(MX8[:, 8:16].unsqueeze(2), g3, axis=AX.X))
    i0 = ve.tensor_tensor(g3, g3, MX8[:, 8:16].unsqueeze(2).to_broadcast(
        [128, NSLAB, 16]), ALU.subtract)
    inc(S("sEV"), i0)
    w(sc, S("sEV"))
    inc(S("sEXP"), sc.activation(GLE[:, :], GLT[:, :], AF.Exp))
    w(ve, S("sEXP"))
    e3 = GLE.rearrange("p (s e) -> p s e", s=NSLAB)
    vsync(ve.reduce_sum(MX8[:, 8:16].unsqueeze(2), e3, axis=AX.X))
    vsync(ve.reciprocal(MX8[:, 8:16], MX8[:, 8:16]))
    vsync(ve.tensor_tensor(e3, e3, MX8[:, 8:16].unsqueeze(2).to_broadcast(
        [128, NSLAB, 16]), ALU.mult))
    for j in range(NSLAB):
        tile = GLE[:, j * 16:(j + 1) * 16]
        vsync(ve.max(out=mx8s, in_=tile))
        vsync(ve.tensor_scalar(MSK[:, :], tile, mx8s[:, 5:6], None, ALU.is_ge))
        i0 = ve.tensor_mul(CMBb[:, j * 16:(j + 1) * 16], MSK[:, :], tile)
    inc(S("sEV"), i0)
    emit_tap("logits", GLE[:, :], ["sEV"])
    w(te, S("sEV"))
    ps7b = PS[7].bitcast(BF16)
    for j in range(NSLAB):
        mm = te.transpose(ps7b[0:16, j * 128:(j + 1) * 128],
                          CMBb[:, j * 16:(j + 1) * 16], identb)
    inc(S("sPE"), mm)
    w(ve, S("sPE"))
    inc(S("sCMB"), ve.tensor_copy(CMF[:, :], ps7b[0:16, 0:1024]))
    join(ALL_ENG, ["sCMB"])
    emit_tap("cmb", CMF[:, :], ["sCMB"])
    stP7.close()
    if stop == "P7":
        return finish()

    ph("P8_shared")
    join((te, ve, sc, gp, sy), ["sXSP"])
    stP8 = mkstack()
    MID = sbuf("MID", [128, NKS * T], BF16, stP8)
    SG = [sbuf(f"SGs{i}", [128, 1024], BF16, stP8) for i in range(2)]

    def midchunk(k, nh=None):
        if nh is None:
            return MID[:, k * T:(k + 1) * T]
        return MID[:, k * T + nh * 512: k * T + (nh + 1) * 512]

    def run_up(blocks_g, blocks_u, nk, rhs_fn, consume_pair):
        npr = len(blocks_g)
        w(te, S("sEV"))
        w(gp, S("sPE"))
        wsem = (S("sW"), S("sW2"))
        bwp = (wsem[0].n, wsem[1].n)
        bp, be = S("sPE").n, S("sEV").n
        for b in range(2 * npr):
            blk = blocks_g[b // 2] if b % 2 == 0 else blocks_u[b // 2]
            if b >= 2:
                w(gp, S("sPE"), bp + b - 1)
            inc(wsem[b % 2], gp.dma_start(
                WB[b % 2][:, 0:nk * 128], blk, max_dma_last_dim=8192), 16)
        for j in range(npr):
            q = 4 * (j % 2)
            if j >= 2:
                w(te, S("sEV"), be + j - 1)
            for half in range(2):
                b = 2 * j + half
                w(te, wsem[b % 2], bwp[b % 2] + 16 * (b // 2 + 1))
                pq = q + 2 * half
                for k in range(nk):
                    lhs = WB[b % 2][:, k * 128:(k + 1) * 128]
                    te.matmul(PS[pq], lhs, rhs_fn(k, 0), start=(k == 0),
                              stop=(k == nk - 1))
                    mm = te.matmul(PS[pq + 1], lhs, rhs_fn(k, 1), start=(k == 0),
                                   stop=(k == nk - 1))
                inc(S("sPE"), mm)
        for j in range(npr):
            consume_pair(j, bp, be)

    bsg8 = S("sSG").n

    def sh_consume(j, bp, be):
        q = 4 * (j % 2)
        w(sc, S("sPE"), bp + 2 * j + 1)
        if j >= 2:
            w(sc, S("sEV"), be + j - 1)
        sc.activation(SG[j % 2][:, 0:512], PS[q], AF.Silu)
        i = sc.activation(SG[j % 2][:, 512:1024], PS[q + 1], AF.Silu)
        inc(S("sSG"), i)
        w(ve, S("sPE"), bp + 2 * j + 2)
        w(ve, S("sSG"), bsg8 + j + 1)
        ve.tensor_tensor(midchunk(j, 0), SG[j % 2][:, 0:512], PS[q + 2], ALU.mult)
        i = ve.tensor_tensor(midchunk(j, 1), SG[j % 2][:, 512:1024], PS[q + 3],
                             ALU.mult)
        inc(S("sEV"), i)

    run_up([sw1_d[j] for j in range(NKS)], [sw1_d[NKS + j] for j in range(NKS)],
           NCH, achunk, sh_consume)
    join(ALL_ENG, ["sEV"])
    emit_tap("mid", MID[:, :], ["sEV"])

    def sh_dn_consume(m, bp, be):
        w(ve, S("sPE"), bp + m + 1)
        ve.tensor_copy(moechunk(m)[:, 0:512], PS[2 * (m % 2)])
        i = ve.tensor_copy(moechunk(m)[:, 512:1024], PS[2 * (m % 2) + 1])
        inc(S("sEV"), i)

    run_proj([sw2_d[m] for m in range(NCH)], NKS, midchunk, sh_dn_consume)
    join(ALL_ENG, ["sEV"])
    stP8.close()
    if stop == "P8":
        return finish()

    ph("P9_experts")
    stP9 = mkstack()
    AE = [sbuf(f"AEs{i}", [128, NKE * T], BF16, stP9) for i in range(2)]
    SG9 = [sbuf(f"SG9s{i}", [128, 1024], BF16, stP9) for i in range(2)]
    CME = [sbuf(f"CMEs{i}", [128, 1024], BF16, stP9) for i in range(2)]

    join((te,), ["sCAST"])
    for ep in range(E // 2):
        e0, e1 = 2 * ep, 2 * ep + 1
        w(te, S("sCMB"))
        if ep >= 1:
            w(te, S("sEV"))
        for i_e, e in enumerate((e0, e1)):
            if i_e == 1:
                w(te, S("sCMB"))
            te.matmul(PS[6][:, :], sel_sb[:, e * 128:(e + 1) * 128],
                      CMF[:, 0:512], start=True, stop=True)
            mm = te.matmul(PS[7][:, :], sel_sb[:, e * 128:(e + 1) * 128],
                           CMF[:, 512:1024], start=True, stop=True)
            inc(S("sPE"), mm)
            w(sc, S("sPE"))
            sc.copy(CME[i_e][:, 0:512], PS[6][:, :])
            i0 = sc.copy(CME[i_e][:, 512:1024], PS[7][:, :])
            inc(S("sCMB"), i0)
        join((te, ve), ["sCMB"])

        for i_e, e in enumerate((e0, e1)):
            bsg = S("sSG").n

            def up_consume(j, bp, be, i_e=i_e, bsg=bsg):
                q = 4 * (j % 2)
                w(sc, S("sPE"), bp + 2 * j + 1)
                if j >= 2:
                    w(sc, S("sEV"), be + j - 1)
                sc.activation(SG9[j % 2][:, 0:512], PS[q], AF.Silu)
                i = sc.activation(SG9[j % 2][:, 512:1024], PS[q + 1], AF.Silu)
                inc(S("sSG"), i)
                w(ve, S("sPE"), bp + 2 * j + 2)
                w(ve, S("sSG"), bsg + j + 1)
                w(ve, S("sEV"), be + j)
                ve.tensor_tensor(CM0[:, 0:512], SG9[j % 2][:, 0:512], PS[q + 2],
                                 ALU.mult)
                ve.tensor_tensor(CM0[:, 512:1024], SG9[j % 2][:, 512:1024],
                                 PS[q + 3], ALU.mult)
                i = ve.tensor_mul(AE[i_e][:, j * T:(j + 1) * T], CM0[:, :],
                                  CME[i_e][:, :])
                inc(S("sEV"), i)

            run_up([w1_d[e, j] for j in range(NKE)],
                   [w1_d[e, NKE + j] for j in range(NKE)],
                   NCH, achunk, up_consume)

        w(te, S("sEV"))
        w(sy, S("sDN"))
        wdsem = (S("sWD"), S("sWD2"), S("sWD3"))
        bwdp = (wdsem[0].n, wdsem[1].n, wdsem[2].n)
        bdn = S("sDN").n
        bev2 = S("sEV2").n
        blocks = [(e_i, m) for m in range(NCH) for e_i in range(2)]
        for b, (e_i, m) in enumerate(blocks):
            if b >= 3:
                w(sy, S("sDN"), bdn + b - 2)
            e = (e0, e1)[e_i]
            inc(wdsem[b % 3], sy.dma_start(
                WD[b % 3][:, :], w2_d[e, m], max_dma_last_dim=8192), 16)
        for m in range(NCH):
            q = 2 * (m % 2)
            if m >= 2:
                w(te, S("sEV2"), bev2 + m - 1)
            for e_i in range(2):
                b = 2 * m + e_i
                w(te, wdsem[b % 3], bwdp[b % 3] + 16 * (b // 3 + 1))
                for k in range(NKE):
                    lhs = WD[b % 3][:, k * 128:(k + 1) * 128]
                    te.matmul(PS[q], lhs,
                              AE[e_i][:, k * T: k * T + 512],
                              start=(e_i == 0 and k == 0),
                              stop=(e_i == 1 and k == NKE - 1))
                    mm = te.matmul(PS[q + 1], lhs,
                                   AE[e_i][:, k * T + 512: (k + 1) * T],
                                   start=(e_i == 0 and k == 0),
                                   stop=(e_i == 1 and k == NKE - 1))
                inc(S("sDN"), mm)
        for m in range(NCH):
            w(ve, S("sDN"), bdn + 2 * (m + 1))
            q = 2 * (m % 2)
            ve.tensor_tensor(moechunk(m)[:, 0:512], moechunk(m)[:, 0:512],
                             PS[q], ALU.add)
            i = ve.tensor_tensor(moechunk(m)[:, 512:1024],
                                 moechunk(m)[:, 512:1024], PS[q + 1], ALU.add)
            inc(S("sEV2"), i)
        join((te,), ["sEV2"])
    join(ALL_ENG, ["sEV2"])
    stP9.close()
    emit_tap("moe", MOE[:, :], ["sEV2"])
    if stop == "P9":
        return finish()

    ph("P10_postLN")
    stP10 = mkstack()
    FF0 = sbuf("FF0", [128, 1024], F32, stP10)
    FF1 = sbuf("FF1", [128, 1024], F32, stP10)
    TP = sbuf("TP", [128, DM], F32, stP10)

    bst10 = S("sST").n
    bc10 = S("sCV").n
    for c in range(NCH):
        if c >= 2:
            w(ve, S("sST"), bst10 + (c - 1))
        inc(S("sCV"), ve.tensor_copy(CM0 if c % 2 == 0 else CM1, moechunk(c)))

    def mchunk_full(c):
        return CM0 if c % 2 == 0 else CM1

    emit_ln_fields(lambda c: mchunk_full(c)[:, :],
                   lambda c, nh: mchunk_full(c)[:, nh * 512:(nh + 1) * 512],
                   "sCV", bc10, double=False, f32_fields=(FF0, FF1))
    join((ve, te), ["sFLD"])
    if stop == "P10":
        return finish()

    ph("P11_out")
    stP11 = mkstack()
    TMo = [sbuf(f"TMo{i}", [128, DM], F32, stP11) for i in range(2)]
    XSL = [sbuf(f"XSL{i}", [128, NCH * 128], F32, stP11) for i in range(2)]
    moe3 = MOE.rearrange("p (c t) -> p c t", c=NCH)
    xsp3 = x_spill.rearrange("p (c t) -> p c t", c=NCH)
    bpe11 = S("sPE").n
    bev11 = S("sEV").n
    qsem3 = (S("sXL"), S("sXL2"))
    bq11p = (qsem3[0].n, qsem3[1].n)
    osem = (S("sOUT"), S("sOUT2"))
    boutp = (osem[0].n, osem[1].n)
    badd = S("sEV2").n
    w(gp, S("sXSP"))
    for j in range(NSLAB):
        if j >= 2:
            w(gp, S("sPE"), bpe11 + j - 1)
        inc(qsem3[j % 2], gp.dma_start(
            XSL[j % 2].rearrange("p (c t) -> p c t", c=NCH),
            xsp3[:, :, j * 128:(j + 1) * 128]), 16)
    for j in range(NSLAB):
        jsl = slice(j * 128, (j + 1) * 128)
        TPv = TP.rearrange("p (c t) -> p c t", c=NCH)
        w(ve, qsem3[j % 2], bq11p[j % 2] + 16 * (j // 2 + 1))
        if j >= 2:
            w(ve, S("sPE"), bpe11 + j - 1)
        ve.tensor_tensor(TPv, moe3[:, :, jsl],
                         FF0[:, jsl].unsqueeze(1).to_broadcast([128, NCH, 128]),
                         ALU.mult)
        ve.tensor_tensor(TPv, TPv,
                         FF1[:, jsl].unsqueeze(1).to_broadcast([128, NCH, 128]),
                         ALU.subtract)
        i0 = ve.tensor_tensor(XSL[j % 2].rearrange("p (c t) -> p c t", c=NCH),
                              XSL[j % 2].rearrange("p (c t) -> p c t", c=NCH),
                              TPv, ALU.add)
        inc(S("sEV2"), i0)
        w(te, S("sEV2"), badd + j + 1)
        if j >= 1:
            w(te, S("sEV"), bev11 + j)
        for c in range(NCH):
            mm = te.transpose(PS[4 + c % 4][:, (c // 4) * 128:(c // 4) * 128 + 128],
                              XSL[j % 2][:, c * 128:(c + 1) * 128], ident)
        inc(S("sPE"), mm)
        w(sc, S("sPE"), bpe11 + j + 1)
        if j >= 2:
            w(sc, osem[j % 2], boutp[j % 2] + 16 * (j // 2))
        for c in range(NCH):
            i0 = sc.copy(TMo[j % 2][:, c * 128:(c + 1) * 128],
                         PS[4 + c % 4][:, (c // 4) * 128:(c // 4) * 128 + 128])
        inc(S("sEV"), i0)
        oq = sy if j % 2 == 0 else sc
        w(oq, S("sEV"), bev11 + j + 1)
        inc(osem[j % 2],
            oq.dma_start(out_d[j * 128:(j + 1) * 128, :], TMo[j % 2]), 16)
    join(ALL_ENG, ["sOUT", "sOUT2"])
    stP11.close()
    stP10.close()
    stCMF.close()
    return finish()



def _mblocks(wT):
    K, M = wT.shape
    nk, nm = K // 128, M // 128
    b = wT.reshape(nk, 128, nm, 128).transpose(2, 1, 0, 3)
    return np.ascontiguousarray(b.reshape(nm, 128, K))


def _prep_inputs(inputs):
    f32 = np.float32
    hs = np.asarray(inputs["L0_attn_hidden_states"], f32).reshape(8192, DM)
    wq = np.asarray(inputs["L0_attn_w_q"], f32)
    wk = np.asarray(inputs["L0_attn_w_k"], f32)
    wv = np.asarray(inputs["L0_attn_w_v"], f32)
    wo = np.asarray(inputs["L0_attn_w_o"], f32)
    gamma = np.asarray(inputs["L0_attn_ln_gamma"], f32)
    beta = np.asarray(inputs["L0_attn_ln_beta"], f32)
    gate_w = np.asarray(inputs["L0_ffn_gate_w"], f32)
    w1 = np.asarray(inputs["L0_ffn_w1"], f32)
    w2 = np.asarray(inputs["L0_ffn_w2"], f32)
    sw1 = np.asarray(inputs["L0_ffn_shared_w1"], f32)
    sw2 = np.asarray(inputs["L0_ffn_shared_w2"], f32)

    com = {}
    com["wq_b"] = _mblocks(wq).astype(BF)
    com["wk_b"] = _mblocks(wk).astype(BF)
    com["wv_b"] = _mblocks(wv).astype(BF)
    com["wo_b"] = _mblocks(wo).astype(BF)
    com["gamma_pp"] = np.ascontiguousarray(gamma.reshape(NCH, 128).T)
    com["beta_pp"] = np.ascontiguousarray(beta.reshape(NCH, 128).T)
    com["gate_wT"] = np.ascontiguousarray(gate_w.T).astype(BF)
    com["sw1_b"] = _mblocks(np.ascontiguousarray(sw1.T)).astype(BF)
    com["sw2_b"] = _mblocks(np.ascontiguousarray(sw2.T)).astype(BF)
    com["w1_b"] = np.stack([_mblocks(np.ascontiguousarray(w1[e].T))
                            for e in range(E)]).astype(BF)
    com["w2_b"] = np.stack([_mblocks(np.ascontiguousarray(w2[e].T))
                            for e in range(E)]).astype(BF)
    com["ident"] = np.eye(128, dtype=f32)
    com["ones_in"] = np.ones((128, 128), dtype=f32)
    sel = np.zeros((E, E * 128), dtype=f32)
    for e in range(E):
        sel[e, e * 128:(e + 1) * 128] = 1.0
    com["sel"] = sel.astype(BF)

    inv_freq = (1.0 / (10000.0 ** (np.arange(0, D, 2, dtype=f32) / D))).astype(f32)
    ang = np.arange(8192, dtype=f32)[:, None] * inv_freq[None, :]
    cos_a = np.cos(ang).astype(f32)
    sin_a = np.sin(ang).astype(f32)
    sgn = np.where((np.arange(128) % 64) < 32, -1.0, 1.0).astype(f32)

    in_maps = []
    for i in range(NCORES):
        s0 = i * T
        m = dict(com)
        m["x_in"] = np.ascontiguousarray(hs[s0:s0 + T, :])
        NH = H * T // 2
        cc = cos_a[s0:s0 + T, :].T.reshape(H, T // 128, 2, 64)
        ss = sin_a[s0:s0 + T, :].T.reshape(H, T // 128, 2, 64)
        cosf = np.concatenate(
            [np.broadcast_to(cc[:, :, 0, :].reshape(1, NH), (64, NH)),
             np.broadcast_to(cc[:, :, 1, :].reshape(1, NH), (64, NH))], axis=0)
        sinf = np.concatenate(
            [np.broadcast_to(ss[:, :, 0, :].reshape(1, NH), (64, NH)),
             np.broadcast_to(ss[:, :, 1, :].reshape(1, NH), (64, NH))], axis=0)
        m["cosf"] = np.ascontiguousarray(cosf).astype(BF)
        m["sinf"] = (sgn[:, None] * sinf).astype(BF)
        in_maps.append(m)
    return in_maps


def kernel(**inputs):
    in_maps = _prep_inputs(inputs)
    nc = build_program()
    res = run_bass_kernel_spmd(nc, in_maps, core_ids=list(range(NCORES)))
    out = np.concatenate([r["out"] for r in res.results], axis=0)
    return out.reshape(1, 8192, DM).astype(np.float32)
